# version 12
# baseline (speedup 1.0000x reference)
import numpy as np
import ml_dtypes
import concourse.bass as bass
import concourse.mybir as mybir
from concourse.bass_utils import run_bass_kernel_spmd

F32 = mybir.dt.float32
BF16 = mybir.dt.bfloat16
AF = mybir.ActivationFunctionType
ALU = mybir.AluOpType
AX = mybir.AxisListType.X
f32 = np.float32

T = 2048
C = 1024
DH = 64
NCH = 8
NNT = 4
NTC = 16
NPR = 8
EPS = 1e-5

class Prog:

    def __init__(self):
        self.ops = {e: [] for e in ("t", "v", "s", "y", "g")}
        self.cnt = {e: 0 for e in ("t", "v", "s", "p")}
        self.nd = 0

    def em(self, e, fn, waits=(), tag="", dgroup=None):
        ws = []
        for w in waits:
            if w is None:
                continue
            we, wv = w
            if we == e and e != "p" and wv < self.cnt[e] - 1:
                continue
            if we in ("t", "v", "s"):
                assert wv <= self.cnt[we], f"future {we}:{wv} ({tag})"
            ws.append((we, wv))
        if e in ("y", "g"):
            key = f"d{self.nd}"
            self.nd += 1
            self.ops[e].append([fn, ws, key, 16, tag, dgroup])
            tok = (key, 16)
        elif e == "p":
            self.ops["g"].append([fn, ws, "p", self.cnt["p"] + 1, tag, None])
            self.cnt["p"] += 1
            tok = ("p", self.cnt["p"])
        else:
            self.ops[e].append([fn, ws, e, self.cnt[e] + 1, tag, None])
            self.cnt[e] += 1
            tok = (e, self.cnt[e])
        return tok


def _rope_tables():
    theta = 1.0 / (10000.0 ** (np.arange(0, DH, 2, dtype=np.float64) / DH))
    pos = np.arange(T, dtype=np.float64)
    freqs = pos[:, None] * theta[None, :]
    emb = np.concatenate([freqs, freqs], axis=-1)
    cosT = np.cos(emb).T.astype(np.float32)
    sinT = np.sin(emb).T.astype(np.float32)
    return (np.ascontiguousarray(np.concatenate([cosT, cosT], axis=0)),
            np.ascontiguousarray(np.concatenate([sinT, sinT], axis=0)))


def _perm_r2t():
    R = np.zeros((DH, DH), dtype=np.float32)
    for m in range(DH // 2):
        R[2 * m, 2 * m + 1] = -1.0
        R[2 * m + 1, 2 * m] = 1.0
    R2 = np.zeros((128, 128), dtype=np.float32)
    R2[0:64, 0:64] = R
    R2[64:128, 64:128] = R
    return np.ascontiguousarray(R2.T)


def _epat():
    E = np.zeros((128, 8 * NPR), dtype=np.float32)
    for ch in range(NPR):
        E[:, 8 * ch + ch] = 1.0
    return E


def build_nc(debug=(), upto="ALL", canary=True):
    nc = bass.Bass("TRN2", target_bir_lowering=False)
    p = Prog()
    em = p.em

    xt_h = nc.dram_tensor("xt", [C, T], F32, kind="ExternalInput")
    out_h = nc.dram_tensor("out", [C, T], F32, kind="ExternalOutput")
    wq_h = nc.dram_tensor("wq", [8, 8, 128, 128], F32, kind="ExternalInput")
    wk_h = nc.dram_tensor("wk", [8, 8, 128, 128], F32, kind="ExternalInput")
    wv_h = nc.dram_tensor("wv", [8, 128, 1024], F32, kind="ExternalInput")
    wo_h = nc.dram_tensor("wo", [8, 8, 128, 128], F32, kind="ExternalInput")
    wb_h = nc.dram_tensor("wb", [8, 128, 64], F32, kind="ExternalInput")
    wr_h = nc.dram_tensor("wr", [8, 8, 128, 128], F32, kind="ExternalInput")
    wkf_h = nc.dram_tensor("wkf", [8, 8, 128, 128], F32, kind="ExternalInput")
    wvf_h = nc.dram_tensor("wvf", [8, 8, 128, 128], F32, kind="ExternalInput")
    gam_h = nc.dram_tensor("gam", [128, 8], F32, kind="ExternalInput")
    mur_h = nc.dram_tensor("mur", [128, 8], F32, kind="ExternalInput")
    muk_h = nc.dram_tensor("muk", [128, 8], F32, kind="ExternalInput")
    cos_h = nc.dram_tensor("cos2", [128, T], F32, kind="ExternalInput")
    sin_h = nc.dram_tensor("sin2", [128, T], F32, kind="ExternalInput")
    r2t_h = nc.dram_tensor("r2t", [128, 128], F32, kind="ExternalInput")
    epat_h = nc.dram_tensor("epat", [128, 64], F32, kind="ExternalInput")
    id8_h = nc.dram_tensor("id8", [8, 8], F32, kind="ExternalInput")
    onesr_h = nc.dram_tensor("onesr", [1, 128], F32, kind="ExternalInput")
    spill_h = nc.dram_tensor("xspill", [C, T], F32)

    dbg_h = {}
    for name, shape, dt in (
        ("XBF", [128, 16384], BF16), ("QT", [128, 16384], BF16),
        ("KTT", [128, 16384], BF16), ("VT", [128, 16640], BF16),
        ("AOT", [128, 16384], BF16), ("R1", [128, 16384], F32),
        ("Y", [128, 16384], F32), ("XK", [128, 16384], BF16),
        ("KACT", [128, 16384], BF16), ("XR", [128, 16384], BF16),
        ("NMU", [1, 2048], F32), ("RSTD", [1, 2048], F32),
        ("CVEC", [128, 8], F32), ("QPRE6", [128, 2048], BF16),
        ("WBB", [128, 512], BF16),
        ("WQS6", [128, 1024], BF16), ("WST6", [128, 2048], F32),
    ):
        if name in debug:
            dbg_h[name] = nc.dram_tensor("dbg_" + name, shape, dt,
                                         kind="ExternalOutput")

    A0 = nc.alloc_sbuf_tensor_at
    SB0 = 16512
    def A(name, shape, dt, offset):
        return A0(name, shape, dt, offset=SB0 + offset)
    XT = A("XT", [128, 16384], F32, offset=0)
    QT = A("QT", [128, 16384], BF16, offset=0)
    KTT = A("KTT", [128, 16384], BF16, offset=32768)
    XT2 = A("XT2", [128, 16384], F32, offset=0)
    XBF = A("XBF", [128, 16384], BF16, offset=65536)
    AOT = A("AOT", [128, 16384], BF16, offset=65536)
    XR = A("XR", [128, 16384], BF16, offset=65536)
    VT = A("VT", [128, 16640], BF16, offset=98304)
    XK = A("XK", [128, 16384], BF16, offset=98304)
    WVB = A("WVB", [128, 8192], BF16, offset=131584)
    KACT = A("KACT", [128, 16384], BF16, offset=131584)
    BOUNCE = A("BOUNCE", [1, 1024], F32, offset=131584)
    CPROW = A("CPROW", [1, 16], F32, offset=135616)
    NMU = A("NMU", [1, 2048], F32, offset=147968)
    NCKV = A("NCKV", [1, 2048], F32, offset=156160)
    WST = A("WST", [128, 2048], F32, offset=164352)
    COS = A("COS", [128, 2048], F32, offset=172544)
    SIN = A("SIN", [128, 2048], F32, offset=180736)
    MU2 = A("MU2", [1, 2048], F32, offset=172544)
    RSTD2 = A("RSTD2", [1, 2048], F32, offset=180736)
    WQS0 = A("WQS0", [128, 1024], BF16, offset=188928)
    WQS1 = A("WQS1", [128, 1024], BF16, offset=190976)
    WKS0 = A("WKS0", [128, 1024], BF16, offset=193024)
    WKS1 = A("WKS1", [128, 1024], BF16, offset=195072)
    KWB0, KWB1 = WQS0, WQS1
    RWB, VWB = WKS0, WKS1
    QPRE = A("QPRE", [128, 2048], BF16, offset=197120)
    QPRE2 = A("QPRE2", [128, 2048], BF16, offset=98304)
    MIXT = A("MIXT", [128, 512], F32, offset=197120)
    T0A = A("T0A", [128, 512], F32, offset=201216)
    T0B = A("T0B", [128, 512], F32, offset=203264)
    T0W = A("T0W", [128, 1024], F32, offset=201216)
    PBIG = [A(f"PBIG{i}", [128, 1024], BF16, offset=205312 + 2048 * i)
            for i in range(2)]
    PBIG += [A(f"PBIG{i + 2}", [128, 1024], BF16, offset=135680 + 2048 * i)
             for i in range(2)]
    PBUF = [PBIG[i // 2][:, 512 * (i % 2):512 * (i % 2) + 512]
            for i in range(8)]
    GAMC = A("GAMC", [128, 8], F32, offset=209408)
    MURC = A("MURC", [128, 8], F32, offset=209440)
    MUKC = A("MUKC", [128, 8], F32, offset=209472)
    RSTDC = A("RSTDC", [128, 16], F32, offset=209504)
    CVEC = A("CVEC", [128, 8], F32, offset=209568)
    CP8 = A("CP8", [8, 2], F32, offset=209600)
    Q2M = A("Q2M", [8, 8], F32, offset=209632)
    Z2M = A("Z2M", [8, 8], F32, offset=209664)
    ONESR = A("ONESR", [1, 128], F32, offset=209696)
    ONESCB = A("ONESCB", [128, 1], BF16, offset=210208)
    ONESCF = A("ONESCF", [128, 1], F32, offset=211712)
    ID8 = A("ID8", [8, 8], F32, offset=210240)
    R2TB = A("R2TB", [128, 128], BF16, offset=210272)
    EPATB = A("EPATB", [128, 64], BF16, offset=210528)
    WBB = A("WBB", [128, 512], BF16, offset=210656)

    PSA = nc.alloc_psum_tensor("psA", [128, 1024], F32)
    PSB = nc.alloc_psum_tensor("psB", [128, 1024], F32)
    PSC = nc.alloc_psum_tensor("psC", [128, 1024], F32)
    PSD = nc.alloc_psum_tensor("psD", [128, 1024], F32)
    PS = [PSA[:, 0:512], PSA[:, 512:1024], PSB[:, 0:512], PSB[:, 512:1024],
          PSC[:, 0:512], PSC[:, 512:1024], PSD[:, 0:512], PSD[:, 512:1024]]
    PSAB = [PSA, PSB]

    def chsl(buf, ch, nt=None):
        if nt is None:
            return buf[:, 2048 * ch: 2048 * ch + 2048]
        return buf[:, 2048 * ch + 512 * nt: 2048 * ch + 512 * nt + 512]

    def mm(psap, lhsT, rhs, start, stop, waits=(), tp=None, tag=""):
        return em("t", lambda t, a=psap, l=lhsT, r=rhs, s0=start, s1=stop, tpp=tp:
                  t.matmul(a, l, r, start=s0, stop=s1, tile_position=tpp),
                  waits=waits, tag=tag)

    cp_cos = em("g", lambda g: g.dma_start(out=COS[:], in_=cos_h[:]), dgroup="tbl")
    cp_sin = em("g", lambda g: g.dma_start(out=SIN[:], in_=sin_h[:]), dgroup="tbl")
    cp_gam = em("y", lambda y: y.dma_start(out=GAMC[:], in_=gam_h[:]), dgroup="c0a")
    cp_mur = em("y", lambda y: y.dma_start(out=MURC[:], in_=mur_h[:]), dgroup="c0a")
    cp_muk = em("y", lambda y: y.dma_start(out=MUKC[:], in_=muk_h[:]), dgroup="c0b")
    cp_id8 = em("y", lambda y: y.dma_start(out=ID8[:], in_=id8_h[:]), dgroup="c0b")
    cp_onr = em("y", lambda y: y.dma_start(out=ONESR[:], in_=onesr_h[:]))
    cp_r2ts = em("y", lambda y: y.dma_start(out=WST[:, 0:128], in_=r2t_h[:]), dgroup="c0c")
    cp_epts = em("y", lambda y: y.dma_start(out=WST[:, 128:192], in_=epat_h[:]), dgroup="c0c")
    cp_r2tb = em("v", lambda v: v.tensor_copy(out=R2TB[:], in_=WST[:, 0:128]),
                 waits=[cp_r2ts])
    cp_eptb = em("v", lambda v: v.tensor_copy(out=EPATB[:], in_=WST[:, 128:192]),
                 waits=[cp_epts])
    cp_onescb = em("v", lambda v: v.memset(ONESCB[:], 1.0))
    cp_onescf = em("v", lambda v: v.memset(ONESCF[:], 1.0))
    cp_can = None
    if canary:
        em("v", lambda v: v.memset(QPRE[:], 0.0))
        em("v", lambda v: v.memset(QPRE2[:], 0.0))
        for _pb in PBIG:
            em("v", lambda v, _pb=_pb: v.memset(_pb[:], 0.0))
        em("v", lambda v: v.memset(T0A[:], 0.0))
        cp_can = em("v", lambda v: v.memset(T0B[:], 0.0))
    cp_wbs = []
    for pr in range(NPR):
        cpw = em("y", lambda y, pr=pr: y.dma_start(
            out=WST[:, 192 + 64 * pr: 256 + 64 * pr], in_=wb_h[pr]),
            dgroup=f"wb{pr // 2}")
        cp_wbs.append(em("v", lambda v, pr=pr: v.tensor_copy(
            out=WBB[:, 64 * pr: 64 * pr + 64],
            in_=WST[:, 192 + 64 * pr: 256 + 64 * pr]), waits=[cpw]))
    cp_wst_free = [cp_r2tb, cp_eptb] + cp_wbs

    cp_x = [em("g", lambda g, ch=ch: g.dma_start(
        out=chsl(XT, ch), in_=xt_h[128 * ch:128 * ch + 128, :]),
        tag=f"xin{ch}", dgroup=f"xin{ch // 2}")
        for ch in range(NCH)]

    cp_sq2 = [[None] * NNT for _ in range(NCH)]
    cp_sx = [[None] * NNT for _ in range(NCH)]
    for ch in range(NCH):
        for nt in range(NNT):
            w = [cp_x[ch]]
            if ch > 0:
                w.append(cp_sq2[ch - 1][nt])
            sq = em("s", lambda s, ch=ch, nt=nt: s.square(
                out=PBUF[nt][:], in_=chsl(XT, ch, nt)),
                waits=w + ([cp_can] if ch == 0 else []),
                tag=f"asq_{ch}{nt}")
            cp_sx[ch][nt] = mm(PS[nt][0:1, :], ONESCF[:, 0:1],
                               chsl(XT, ch, nt),
                               start=(ch == 0), stop=(ch == NCH - 1),
                               waits=[cp_x[ch], cp_onescf], tag=f"asx{ch}{nt}")
            cp_sq2[ch][nt] = mm(PS[4 + nt][0:1, :], ONESCB[:, 0:1], PBUF[nt][:],
                                start=(ch == 0), stop=(ch == NCH - 1),
                                waits=[sq, cp_onescb], tag=f"asq{ch}{nt}")

    cp_nmu = [None] * NNT
    cp_rstd = [None] * NNT
    cp_c3 = [None] * NNT
    for nt in range(NNT):
        wstat = [cp_sq2[NCH - 1][nt]]
        cp_nmu[nt] = em("s", lambda s, nt=nt: s.mul(
            out=NMU[0:1, 512 * nt:512 * nt + 512], in_=PS[nt][0:1, :],
            mul=f32(-1.0 / C)), waits=wstat, tag=f"nmu{nt}")
        c1 = em("v", lambda v, nt=nt: v.tensor_tensor(
            out=T0A[0:1, :], in0=NMU[0:1, 512 * nt:512 * nt + 512],
            in1=NMU[0:1, 512 * nt:512 * nt + 512], op=ALU.mult),
            waits=[cp_nmu[nt]], tag=f"bmu2{nt}")
        c2 = em("s", lambda s, nt=nt: s.mul(
            out=T0B[0:1, :], in_=PS[4 + nt][0:1, :], mul=f32(1.0 / C)),
            waits=wstat + [cp_rstd[nt - 1] if nt else cp_can], tag=f"bmsq{nt}")
        cp_c3[nt] = em("v", lambda v: v.tensor_tensor(
            out=T0B[0:1, :], in0=T0B[0:1, :], in1=T0A[0:1, :],
            op=ALU.subtract), waits=[c1, c2], tag=f"bvar{nt}")
        c4 = em("s", lambda s: s.activation(
            out=T0B[0:1, :], in_=T0B[0:1, :], func=AF.Sqrt, bias=f32(EPS)),
            waits=[cp_c3[nt]], tag=f"bsd{nt}")
        cp_rstd[nt] = em("v", lambda v, nt=nt: v.reciprocal(
            out=NCKV[0:1, 512 * nt:512 * nt + 512], in_=T0B[0:1, :]),
            waits=[c4], tag=f"brstd{nt}")
    cp_fold_sin = [None] * NNT
    cp_fold = []
    cp_xbf = []
    cp_spill = [None] * NCH
    cp_mub = [None] * NNT
    for nt in range(NNT):
        rb = mm(PS[nt][:, :], ONESR[0:1, 0:128],
                NCKV[0:1, 512 * nt:512 * nt + 512], start=True, stop=True,
                waits=[cp_rstd[nt], cp_onr, cp_nmu[nt]], tag=f"brb{nt}")
        fc = em("v", lambda v, nt=nt: v.tensor_tensor(
            out=COS[:, 512 * nt:512 * nt + 512],
            in0=COS[:, 512 * nt:512 * nt + 512], in1=PS[nt][:, :],
            op=ALU.mult), waits=[rb, cp_cos], tag=f"bfc{nt}")
        fs = em("v", lambda v, nt=nt: v.tensor_tensor(
            out=SIN[:, 512 * nt:512 * nt + 512],
            in0=SIN[:, 512 * nt:512 * nt + 512], in1=PS[nt][:, :],
            op=ALU.mult), waits=[rb, cp_sin], tag=f"bfs{nt}")
        cp_fold_sin[nt] = fs
        cp_fold += [fc, fs]
        cp_mub[nt] = mm(PS[4 + nt][:, :], ONESR[0:1, 0:128],
                        NMU[0:1, 512 * nt:512 * nt + 512], start=True,
                        stop=True,
                        waits=[cp_nmu[nt], cp_rstd[nt], cp_onr],
                        tag=f"bmub{nt}")
    xbf_by_ch = [[None] * NNT for _ in range(NCH)]
    for ch in range(NCH):
        for nt in range(NNT):
            t = em("v", lambda v, ch=ch, nt=nt: v.tensor_tensor(
                out=chsl(XBF, ch, nt), in0=chsl(XT, ch, nt),
                in1=PS[4 + nt][:, :], op=ALU.add), waits=[cp_mub[nt]],
                tag=f"xbf{ch}_{nt}")
            xbf_by_ch[ch][nt] = t
            cp_xbf.append(t)
        cp_spill[ch] = em("g", lambda g, ch=ch: g.dma_start(
            out=spill_h[128 * ch:128 * ch + 128, :], in_=chsl(XT, ch)),
            waits=xbf_by_ch[ch], tag=f"spill{ch}", dgroup=f"sp{ch // 2}")
    rc_last = None
    for tc in range(NTC):
        rc_last = mm(PS[0][:, tc:tc + 1], NCKV[0:1, 128 * tc:128 * tc + 128],
                     ONESR[0:1, 0:1], start=True, stop=True,
                     waits=cp_rstd + cp_fold, tag=f"brc{tc}")
    cp_rstdc = em("s", lambda s: s.copy(out=RSTDC[:], in_=PS[0][:, 0:16]),
                  waits=[rc_last], tag="brcev")

    if "NMU" in debug:
        em("g", lambda g: g.dma_start(out=dbg_h["NMU"][:], in_=NMU[0:1, :]),
           waits=cp_nmu)
    if "RSTD" in debug:
        em("g", lambda g: g.dma_start(out=dbg_h["RSTD"][:], in_=NCKV[0:1, :]),
           waits=cp_rstd + [cp_rstdc])
    if "XBF" in debug:
        em("g", lambda g: g.dma_start(out=dbg_h["XBF"][:], in_=XBF[:]),
           waits=cp_xbf)
    if upto == "B":
        return nc, p, dbg_h

    state = {
        "bank_free": {nt: (cp_rstdc if nt == 0 else cp_fold_sin[nt])
                      for nt in range(NNT)},
        "rot_free": {6: cp_xbf[-2], 7: cp_xbf[-1]},
        "qpre_readers": [[[] for _ in range(NNT)] for _ in range(2)],
        "bil_free": {0: None, 1: None},
        "t0_reader": None,
    }

    stage_state = {0: [], 1: []}

    def proj_qk(which, out_tokens):
        wtile_h = wq_h if which == "q" else wk_h
        slots = (WQS0, WQS1) if which == "q" else (WKS0, WKS1)
        slot_readers = {0: [], 1: []}
        stage_readers = stage_state
        for oc in range(NCH):
            par = oc % 2
            slot = slots[par]
            half = par * 1024
            stg = WST[:, half:half + 1024].rearrange("p (c j) -> p c j", j=128)
            wd = stage_readers[par] + (cp_wst_free if which == "q" and oc < 2 else [])
            cp_d = em("y", lambda y, stg=stg, oc=oc: y.dma_start(
                out=stg, in_=wtile_h[oc].rearrange("c p j -> p c j")),
                waits=wd, tag=f"{which}wd{oc}", dgroup=f"{which}w{oc // 2}")
            casts = []
            for cc in range(NCH):
                casts.append(em("v", lambda v, slot=slot, cc=cc, half=half:
                               v.tensor_scalar_mul(
                    out=slot[:, 128 * cc:128 * cc + 128],
                    in0=WST[:, half + 128 * cc: half + 128 * cc + 128],
                    scalar1=GAMC[:, cc:cc + 1]),
                    waits=[cp_d, cp_gam] + slot_readers[par],
                    tag=f"{which}wc{oc}_{cc}"))
            stage_readers[par] = [casts[-1]]
            reads = []
            last = [None] * NNT
            for cc in range(NCH):
                for nt in range(NNT):
                    w = [casts[cc]]
                    if cc == 0:
                        w.append(state["bank_free"][nt])
                    tok = mm(PS[nt][:, :], slot[:, 128 * cc:128 * cc + 128],
                             chsl(XBF, cc, nt), start=(cc == 0),
                             stop=(cc == NCH - 1),
                             waits=w, tag=f"{which}p{oc}_{cc}_{nt}")
                    last[nt] = tok
                    reads.append(tok)
            slot_readers[par] = reads
            qp = (QPRE, QPRE2)[par]
            for nt in range(NNT):
                cp_ev = em("s", lambda s, nt=nt, qp=qp: s.copy(
                    out=qp[:, 512 * nt:512 * nt + 512], in_=PS[nt][:, :]),
                    waits=[last[nt]] + state["qpre_readers"][par][nt],
                    tag=f"{which}ev{oc}_{nt}")
                state["qpre_readers"][par][nt] = []
                state["bank_free"][nt] = cp_ev
                rbk = 6 + nt % 2
                rot = mm(PS[rbk][0:128, :], R2TB[:],
                         qp[:, 512 * nt:512 * nt + 512], start=True, stop=True,
                         waits=[cp_ev, cp_r2tb, state["rot_free"][rbk]],
                         tag=f"{which}rot{oc}_{nt}")
                tt1 = em("v", lambda v, nt=nt, qp=qp: v.tensor_tensor(
                    out=T0A[:], in0=qp[:, 512 * nt:512 * nt + 512],
                    in1=COS[:, 512 * nt:512 * nt + 512], op=ALU.mult),
                    waits=[cp_ev, state["t0_reader"]] +
                          (cp_fold if oc == 0 else []),
                    tag=f"{which}t1_{oc}_{nt}")
                state["qpre_readers"][par][nt].append(tt1)
                tt2 = em("v", lambda v, rbk=rbk, nt=nt: v.tensor_tensor(
                    out=T0B[:], in0=PS[rbk][0:128, :],
                    in1=SIN[:, 512 * nt:512 * nt + 512], op=ALU.mult),
                    waits=[rot], tag=f"{which}t2_{oc}_{nt}")
                state["rot_free"][rbk] = tt2
                if which == "q":
                    tokq = em("v", lambda v, oc=oc, nt=nt: v.tensor_tensor(
                        out=chsl(QT, oc, nt), in0=T0A[:], in1=T0B[:], op=ALU.add),
                        waits=[tt1, tt2, cp_spill[oc // 2]],
                        tag=f"qsum{oc}_{nt}")
                    state["t0_reader"] = tokq
                    out_tokens.append(tokq)
                else:
                    kslot = nt % 2
                    kr = em("v", lambda v, kslot=kslot: v.tensor_tensor(
                        out=PBUF[kslot][:], in0=T0A[:], in1=T0B[:], op=ALU.add),
                        waits=[tt1, tt2, state["bil_free"][kslot]],
                        tag=f"krop{oc}_{nt}")
                    state["t0_reader"] = kr
                    b0 = mm(PS[nt][0:64, :], WBB[0:64, 64 * oc:64 * oc + 64],
                            PBUF[kslot][0:64, :], start=True, stop=True,
                            waits=[kr, cp_ev] + (cp_wbs if oc == 0 else []),
                            tp=(0, 0), tag=f"bil0_{oc}_{nt}")
                    b1 = mm(PS[nt][64:128, :], WBB[64:128, 64 * oc:64 * oc + 64],
                            PBUF[kslot][64:128, :], start=True, stop=True,
                            waits=[kr], tp=(64, 64), tag=f"bil1_{oc}_{nt}")
                    state["bil_free"][kslot] = b1
                    ktev = em("s", lambda s, oc=oc, nt=nt: s.copy(
                        out=chsl(KTT, oc, nt), in_=PS[nt][0:128, :]),
                        waits=[b0, b1, cp_spill[4 + oc // 2]],
                        tag=f"ktev{oc}_{nt}")
                    state["bank_free"][nt] = ktev
                    out_tokens.append(ktev)
            if which == "q" and oc == 0 and "QPRE6" in debug:
                em("g", lambda g: g.dma_start(out=dbg_h["QPRE6"][:], in_=QPRE[:]),
                   waits=[state["bank_free"][n] for n in range(NNT)])
            if which == "q" and oc == 0 and "WQS6" in debug:
                em("g", lambda g, slot=slot: g.dma_start(
                    out=dbg_h["WQS6"][:], in_=slot[:]), waits=slot_readers[par])
            if which == "q" and oc == 0 and "WST6" in debug:
                em("g", lambda g: g.dma_start(
                    out=dbg_h["WST6"][:], in_=WST[:]), waits=[casts[-1]])
        return out_tokens

    cp_q = proj_qk("q", [])
    cp_k = proj_qk("k", [])

    if "QT" in debug:
        em("g", lambda g: g.dma_start(out=dbg_h["QT"][:], in_=QT[:]), waits=cp_q)
    if "KTT" in debug:
        em("g", lambda g: g.dma_start(out=dbg_h["KTT"][:], in_=KTT[:]), waits=cp_k)
    if "WBB" in debug:
        em("g", lambda g: g.dma_start(out=dbg_h["WBB"][:], in_=WBB[:]),
           waits=cp_k)
    if upto == "C":
        return nc, p, dbg_h

    cp_wv_cast = []
    stage_readers = {0: list(stage_state[0]), 1: list(stage_state[1])}
    for cc in range(NCH):
        par = cc % 2
        half = par * 1024
        cp_d = em("y", lambda y, cc=cc, half=half: y.dma_start(
            out=WST[:, half:half + 1024], in_=wv_h[cc]),
            waits=stage_readers[par], tag=f"wvd{cc}", dgroup=f"wv{cc // 2}")
        cst = em("s", lambda s, cc=cc, half=half: s.mul(
            out=WVB[:, 1024 * cc:1024 * cc + 1024],
            in_=WST[:, half:half + 1024],
            mul=GAMC[:, cc:cc + 1]), waits=[cp_d], tag=f"wvc{cc}")
        stage_readers[par] = [cst]
        cp_wv_cast.append(cst)
    ones_ap = VT[:, 0:16640].rearrange(
        "p (tc h e) -> p tc h e", h=16, e=65)[:, :, :, 64:65]
    cp_vones = em("v", lambda v: v.memset(ones_ap, 1.0), waits=[cp_q[-1]],
                  tag="vones")
    cp_vt = [cp_vones]
    vev_free = {i: cp_k[-1] for i in range(4)}
    for tc in range(NTC):
        b = (tc % 2) * 2
        last = [None, None]
        for cc in range(NCH):
            for h2 in range(2):
                w = [cp_wv_cast[cc]]
                if cc == 0:
                    w.append(vev_free[b + h2])
                last[h2] = mm(PS[b + h2][:, :],
                              XBF[:, 2048 * cc + 128 * tc:2048 * cc + 128 * tc + 128],
                              WVB[:, 1024 * cc + 512 * h2:1024 * cc + 512 * h2 + 512],
                              start=(cc == 0), stop=(cc == NCH - 1), waits=w,
                              tag=f"vp{tc}_{cc}_{h2}")
        for h2 in range(2):
            dst = VT[:, 1040 * tc: 1040 * tc + 1040].rearrange(
                "p (h e) -> p h e", e=65)[:, 8 * h2:8 * h2 + 8, 0:64]
            src = PS[b + h2][:, :].rearrange("p (h e) -> p h e", e=64)
            cp_ev = em("v", lambda v, dst=dst, src=src, tc=tc, h2=h2:
                       v.tensor_scalar_mul(
                out=dst, in0=src, scalar1=RSTDC[:, tc:tc + 1]),
                waits=[last[h2], cp_rstdc, cp_vones], tag=f"vev{tc}_{h2}")
            vev_free[b + h2] = cp_ev
            cp_vt.append(cp_ev)

    if "VT" in debug:
        em("g", lambda g: g.dma_start(out=dbg_h["VT"][:], in_=VT[:]), waits=cp_vt)
    if upto == "D":
        return nc, p, dbg_h

    red_q = [None] * NNT
    red_z = [None] * NNT
    npat_free = {0: None, 1: None, 2: None, 3: None}
    cp_q2 = [None] * NNT
    cp_z2 = [None] * NNT
    for src, base_ps, cps, idx in ((QT, 0, cp_q2, 0), (KTT, 4, cp_z2, 1)):
        for pr in range(NPR):
            for nt in range(NNT):
                slot = 2 * (nt % 2) + idx
                w = [cp_q[-1], cp_k[-1], cp_eptb, npat_free[slot]]
                if pr == 0 and nt == 0 and idx == 0:
                    w += cp_vt
                sq = em("v", lambda v, src=src, pr=pr, nt=nt, slot=slot:
                        v.tensor_tensor(out=PBUF[slot][:], in0=chsl(src, pr, nt),
                                        in1=chsl(src, pr, nt), op=ALU.mult),
                        waits=w, tag=f"esq{idx}_{pr}_{nt}")
                tok = mm(PS[base_ps + nt][0:8, :], EPATB[:, 8 * pr:8 * pr + 8],
                         PBUF[slot][:], start=(pr == 0), stop=(pr == NPR - 1),
                         waits=[sq] + ([cp_vt[-1]] if pr == 0 and idx == 1 else []),
                         tag=f"ept{idx}_{pr}_{nt}")
                npat_free[slot] = tok
                cps[nt] = tok
    for nt in range(NNT):
        red_q[nt] = em("v", lambda v, nt=nt: v.reduce_max(
            out=Q2M[0:8, nt:nt + 1], in_=PS[nt][0:8, :], axis=AX),
            waits=[cp_q2[nt]], tag=f"redq{nt}")
        red_z[nt] = em("v", lambda v, nt=nt: v.reduce_max(
            out=Z2M[0:8, nt:nt + 1], in_=PS[4 + nt][0:8, :], axis=AX),
            waits=[cp_z2[nt]], tag=f"redz{nt}")
    q1 = em("v", lambda v: v.tensor_tensor(out=Q2M[0:8, 0:1], in0=Q2M[0:8, 0:1],
                                           in1=Q2M[0:8, 1:2], op=ALU.max),
            waits=[red_q[3]])
    q2 = em("v", lambda v: v.tensor_tensor(out=Q2M[0:8, 2:3], in0=Q2M[0:8, 2:3],
                                           in1=Q2M[0:8, 3:4], op=ALU.max),
            waits=[red_q[2], red_q[3]])
    q3 = em("v", lambda v: v.tensor_tensor(out=Q2M[0:8, 0:1], in0=Q2M[0:8, 0:1],
                                           in1=Q2M[0:8, 2:3], op=ALU.max),
            waits=[q1, q2])
    z1 = em("v", lambda v: v.tensor_tensor(out=Z2M[0:8, 0:1], in0=Z2M[0:8, 0:1],
                                           in1=Z2M[0:8, 1:2], op=ALU.max),
            waits=[red_z[3]])
    z2 = em("v", lambda v: v.tensor_tensor(out=Z2M[0:8, 2:3], in0=Z2M[0:8, 2:3],
                                           in1=Z2M[0:8, 3:4], op=ALU.max),
            waits=[red_z[2], red_z[3]])
    z3 = em("v", lambda v: v.tensor_tensor(out=Z2M[0:8, 0:1], in0=Z2M[0:8, 0:1],
                                           in1=Z2M[0:8, 2:3], op=ALU.max),
            waits=[z1, z2])
    e1 = em("v", lambda v: v.tensor_tensor(out=CP8[0:8, 0:1], in0=Q2M[0:8, 0:1],
                                           in1=Z2M[0:8, 0:1], op=ALU.mult),
            waits=[q3, z3], tag="cpmul")
    e2 = em("s", lambda s: s.activation(out=CP8[0:8, 0:1], in_=CP8[0:8, 0:1],
                                        func=AF.Sqrt, scale=f32(1.0 / 64.0)),
            waits=[e1], tag="cpsqrt")
    tr = mm(PS[0][0:1, 0:8], CP8[0:8, 0:1], ID8[:], start=True, stop=True,
            waits=[e2, cp_id8], tag="cptr")
    e3 = em("s", lambda s: s.copy(out=CPROW[0:1, 0:8], in_=PS[0][0:1, 0:8]),
            waits=[tr], tag="cprow")
    br = mm(PS[1][:, 0:8], ONESR[0:1, 0:128], CPROW[0:1, 0:8], start=True,
            stop=True, waits=[e3], tag="cpbr")
    cp_cvec = em("s", lambda s: s.mul(out=CVEC[:], in_=PS[1][:, 0:8],
                                      mul=f32(-1.0)), waits=[br], tag="cvec")
    if "CVEC" in debug:
        em("g", lambda g: g.dma_start(out=dbg_h["CVEC"][:], in_=CVEC[:]),
           waits=[cp_cvec])

    PSAB3 = [PSA, PSB, PSD]
    expdone = {0: cp_cvec, 1: red_q[3], 2: red_z[3]}
    pvprev = {0: None, 1: None, 2: None, 3: None}
    o_free = {0: red_z[0], 1: red_z[1]}
    bounce_free = red_z[2]
    cp_aot = []
    cp_qk_last = None
    for pr in range(NPR):
        for qt in range(NNT):
            exptok = {}
            pv_last = [None, None]

            def emit_pv(kc, hh, pr=pr, qt=qt, pv_last=pv_last, exptok=exptok):
                wpv = [exptok[kc]]
                if kc == 0:
                    wpv.append(o_free[hh])
                pv = mm(PSC[0:65, 512 * hh:512 * hh + 512],
                        VT[:, 1040 * kc + 65 * (2 * pr + hh):
                           1040 * kc + 65 * (2 * pr + hh) + 65],
                        PBUF[2 * (kc % 4) + hh][:], start=(kc == 0),
                        stop=(kc == NTC - 1),
                        waits=wpv, tag=f"pv{pr}_{qt}_{kc}_{hh}")
                pv_last[hh] = pv
                return pv

            for kc in range(NTC):
                par = kc % 3
                slot4 = kc % 4
                qks = []
                for hh in range(2):
                    base = 64 * hh
                    qk = mm(PSAB3[par][:, 512 * hh:512 * hh + 512],
                            KTT[base:base + 64,
                                2048 * pr + 128 * kc:2048 * pr + 128 * kc + 128],
                            QT[base:base + 64,
                               2048 * pr + 512 * qt:2048 * pr + 512 * qt + 512],
                            start=True, stop=True,
                            waits=[expdone[par]],
                            tag=f"qk{pr}_{qt}_{kc}_{hh}")
                    qks.append(qk)
                    cp_qk_last = qk
                pvw = [pvprev[slot4]]
                if kc >= 4:
                    pvw = [emit_pv(kc - 4, 0), emit_pv(kc - 4, 1)]
                ex = em("s", lambda s, slot4=slot4, par=par, pr=pr: s.activation(
                    out=PBIG[slot4][:], in_=PSAB3[par][:, :], func=AF.Exp,
                    scale=f32(0.125), bias=CVEC[:, pr:pr + 1]),
                    waits=qks + [cp_cvec] + pvw,
                    tag=f"ex{pr}_{qt}_{kc}")
                exptok[kc] = ex
                expdone[par] = ex
            for kc in (NTC - 4, NTC - 3, NTC - 2, NTC - 1):
                emit_pv(kc, 0)
                pvprev[kc % 4] = emit_pv(kc, 1)
            rl = em("s", lambda s: s.activation(
                out=BOUNCE[0:1, :], in_=PSC[64:65, 0:1024], func=AF.Ln),
                waits=[pv_last[0], pv_last[1], bounce_free],
                tag=f"rl{pr}_{qt}")
            rc = em("s", lambda s: s.activation(
                out=BOUNCE[0:1, :], in_=BOUNCE[0:1, :], func=AF.Exp,
                scale=f32(-1.0)), waits=[rl], tag=f"rc{pr}_{qt}")
            nbr0 = mm(PSD[0:64, 0:512], ONESR[0:1, 0:64], BOUNCE[0:1, 0:512],
                      start=True, stop=True, waits=[rc, expdone[2]],
                      tag=f"nbr0{pr}_{qt}")
            nbr1 = mm(PSD[0:64, 512:1024], ONESR[0:1, 0:64],
                      BOUNCE[0:1, 512:1024], start=True, stop=True,
                      waits=[rc], tag=f"nbr1{pr}_{qt}")
            cpc = em("v", lambda v: v.tensor_copy(
                out=T0W[0:64, :], in_=PSD[0:64, 0:1024]),
                waits=[nbr0, nbr1, o_free[0]], tag=f"ncp{pr}_{qt}")
            bounce_free = cpc
            expdone[2] = cpc
            for hh in range(2):
                tt = em("v", lambda v, hh=hh, pr=pr, qt=qt: v.tensor_tensor(
                    out=AOT[64 * hh:64 * hh + 64,
                            2048 * pr + 512 * qt:2048 * pr + 512 * qt + 512],
                    in0=PSC[0:64, 512 * hh:512 * hh + 512],
                    in1=T0W[0:64, 512 * hh:512 * hh + 512], op=ALU.mult),
                    waits=[cpc], tag=f"ntt{pr}_{qt}_{hh}")
                o_free[hh] = tt
                cp_aot.append(tt)

    if "AOT" in debug:
        em("g", lambda g: g.dma_start(out=dbg_h["AOT"][:], in_=AOT[:]),
           waits=cp_aot)
    if upto == "E":
        return nc, p, dbg_h

    cp_rel = [em("g", lambda g, ch=ch: g.dma_start(
        out=chsl(XT2, ch), in_=spill_h[128 * ch:128 * ch + 128, :]),
        waits=[cp_qk_last], tag=f"rel{ch}", dgroup=f"rl{ch // 2}")
        for ch in range(NCH)]
    cp_r1 = []
    stageF = {0: [cp_wv_cast[-1]], 1: [cp_wv_cast[-1]]}
    slotF = {0: [], 1: []}
    fev_free = [None] * NNT
    for dm in range(NCH):
        par = dm % 2
        slot = (WQS0, WQS1)[par]
        cp_d = em("y", lambda y, dm=dm, par=par: y.dma_start(
            out=WST[:, par * 1024:par * 1024 + 1024].rearrange(
                "p (c j) -> p c j", j=128),
            in_=wo_h[dm].rearrange("c p j -> p c j")),
            waits=stageF[par] + slotF[par] + (cp_q + cp_k if dm < 2 else []),
            tag=f"wod{dm}", dgroup=f"wo{dm // 2}")
        cst = em("v", lambda v, slot=slot, par=par: v.tensor_copy(
            out=slot[:], in_=WST[:, par * 1024:par * 1024 + 1024]),
            waits=[cp_d], tag=f"woc{dm}")
        stageF[par] = [cst]
        reads = []
        for nt in range(NNT):
            last = None
            for ic in range(NCH):
                w = [cst]
                if ic == 0:
                    w.append(fev_free[nt])
                    if dm == 0:
                        w.append(cp_aot[-1])
                last = mm(PS[nt][:, :], slot[:, 128 * ic:128 * ic + 128],
                          chsl(AOT, ic, nt), start=(ic == 0),
                          stop=(ic == NCH - 1), waits=w, tag=f"wo{dm}_{ic}_{nt}")
                reads.append(last)
            ev = em("v", lambda v, dm=dm, nt=nt: v.tensor_tensor(
                out=chsl(XT2, dm, nt), in0=PS[nt][:, :], in1=chsl(XT2, dm, nt),
                op=ALU.add), waits=[last, cp_rel[dm]], tag=f"fev{dm}_{nt}")
            fev_free[nt] = ev
            cp_r1.append(ev)
        slotF[par] = reads

    if "R1" in debug:
        em("g", lambda g: g.dma_start(out=dbg_h["R1"][:], in_=XT2[:]),
           waits=cp_r1)
    if upto == "F":
        return nc, p, dbg_h

    def layernorm_inplace(prev_last, murow, rstdrow, tag):
        sq2 = [[None] * NNT for _ in range(NCH)]
        for ch in range(NCH):
            for nt in range(NNT):
                w = [prev_last] if ch == 0 else [sq2[ch - 1][nt]]
                sq = em("s", lambda s, ch=ch, nt=nt: s.square(
                    out=PBUF[nt][:], in_=chsl(XT2, ch, nt)), waits=w,
                    tag=f"{tag}cb{ch}{nt}")
                sx = mm(PS[nt][0:1, :], ONESCF[:, 0:1], chsl(XT2, ch, nt),
                        start=(ch == 0), stop=(ch == NCH - 1),
                        waits=[sq if ch else prev_last, cp_onescf],
                        tag=f"{tag}sx{ch}{nt}")
                sq2[ch][nt] = mm(PS[4 + nt][0:1, :], ONESCB[:, 0:1], PBUF[nt][:],
                                 start=(ch == 0), stop=(ch == NCH - 1),
                                 waits=[sq], tag=f"{tag}sq{ch}{nt}")
        cp_mu = [None] * NNT
        cp_rs = [None] * NNT
        c3prev = None
        for nt in range(NNT):
            wstat = [sq2[NCH - 1][nt]]
            cp_mu[nt] = em("s", lambda s, nt=nt, murow=murow: s.mul(
                out=murow[0:1, 512 * nt:512 * nt + 512], in_=PS[nt][0:1, :],
                mul=f32(-1.0 / C)), waits=wstat, tag=f"{tag}mu{nt}")
            a1 = em("v", lambda v, nt=nt, murow=murow: v.tensor_tensor(
                out=T0A[0:1, :], in0=murow[0:1, 512 * nt:512 * nt + 512],
                in1=murow[0:1, 512 * nt:512 * nt + 512], op=ALU.mult),
                waits=[cp_mu[nt]], tag=f"{tag}a1{nt}")
            a2 = em("s", lambda s, nt=nt: s.mul(
                out=T0B[0:1, :], in_=PS[4 + nt][0:1, :], mul=f32(1.0 / C)),
                waits=wstat + [c3prev], tag=f"{tag}a2{nt}")
            a3 = em("v", lambda v: v.tensor_tensor(
                out=T0B[0:1, :], in0=T0B[0:1, :], in1=T0A[0:1, :],
                op=ALU.subtract), waits=[a1, a2], tag=f"{tag}a3{nt}")
            a4 = em("s", lambda s: s.activation(
                out=T0B[0:1, :], in_=T0B[0:1, :], func=AF.Sqrt,
                bias=f32(EPS)), waits=[a3], tag=f"{tag}a4{nt}")
            cp_rs[nt] = em("v", lambda v, nt=nt, rstdrow=rstdrow: v.reciprocal(
                out=rstdrow[0:1, 512 * nt:512 * nt + 512], in_=T0B[0:1, :]),
                waits=[a4], tag=f"{tag}rs{nt}")
            c3prev = cp_rs[nt]
        ap_last = []
        ap_nt = [None] * NNT
        ap_ch = [None] * NCH
        bank_free = {0: cp_mu[0], 1: cp_mu[1], 2: cp_mu[2], 3: cp_mu[3]}
        for nt in range(NNT):
            b = 2 * (nt % 2)
            mub = mm(PS[b][:, :], ONESR[0:1, 0:128],
                     murow[0:1, 512 * nt:512 * nt + 512], start=True, stop=True,
                     waits=[cp_mu[nt], bank_free[b]], tag=f"{tag}mub{nt}")
            rsb = mm(PS[b + 1][:, :], ONESR[0:1, 0:128],
                     rstdrow[0:1, 512 * nt:512 * nt + 512], start=True,
                     stop=True, waits=[cp_rs[nt], bank_free[b + 1]],
                     tag=f"{tag}rsb{nt}")
            s1 = s2 = None
            for ch in range(NCH):
                s1 = em("v", lambda v, ch=ch, nt=nt, b=b: v.tensor_tensor(
                    out=chsl(XT2, ch, nt), in0=chsl(XT2, ch, nt),
                    in1=PS[b][:, :], op=ALU.add), waits=[mub],
                    tag=f"{tag}p1_{ch}_{nt}")
                s2 = em("v", lambda v, ch=ch, nt=nt, b=b: v.tensor_tensor(
                    out=chsl(XT2, ch, nt), in0=chsl(XT2, ch, nt),
                    in1=PS[b + 1][:, :], op=ALU.mult), waits=[rsb, s1],
                    tag=f"{tag}p2_{ch}_{nt}")
                ap_last.append(s2)
                ap_ch[ch] = s2
            ap_nt[nt] = s2
            bank_free[b] = s1
            bank_free[b + 1] = s2
        return ap_last, ap_nt, ap_ch

    cp_y, cp_y_nt, _ = layernorm_inplace(cp_r1[-1], MU2, RSTD2, "ga")
    if "Y" in debug:
        em("g", lambda g: g.dma_start(out=dbg_h["Y"][:], in_=XT2[:]), waits=cp_y)
    if upto == "G":
        return nc, p, dbg_h

    def mix(dst, mucol, cp_mcol, prev_nt, tag):
        cps = []
        cps_nt = [None] * NNT
        for nt in range(NNT):
            for ch in range(NCH):
                base = 2048 * ch + 512 * nt
                lo = 1 if nt == 0 else 0
                d = em("p", lambda g, base=base, lo=lo: g.tensor_tensor(
                    out=MIXT[:, lo:512], in0=XT2[:, base + lo:base + 512],
                    in1=XT2[:, base + lo - 1:base + 511], op=ALU.subtract),
                    waits=[prev_nt[nt], cp_mcol, cps[-1] if cps else None],
                    tag=f"{tag}d{ch}_{nt}")
                tok = em("v", lambda v, base=base, lo=lo, ch=ch, dst=dst,
                         mucol=mucol: v.scalar_tensor_tensor(
                    out=dst[:, base + lo:base + 512], in0=MIXT[:, lo:512],
                    scalar=mucol[:, ch:ch + 1],
                    in1=XT2[:, base + lo - 1:base + 511],
                    op0=ALU.mult, op1=ALU.add), waits=[d], tag=f"{tag}m{ch}_{nt}")
                cps.append(tok)
                if nt == 0:
                    tok = em("v", lambda v, base=base, ch=ch, dst=dst,
                             mucol=mucol: v.tensor_scalar_mul(
                        out=dst[:, base:base + 1], in0=XT2[:, base:base + 1],
                        scalar1=mucol[:, ch:ch + 1]), waits=[cp_mcol],
                        tag=f"{tag}c0{ch}")
                    cps.append(tok)
            cps_nt[nt] = cps[-1]
        return cps, cps_nt

    cp_xk, cp_xk_nt = mix(XK, MUKC, cp_muk, cp_y_nt, "mk")
    if "XK" in debug:
        em("g", lambda g: g.dma_start(out=dbg_h["XK"][:], in_=XK[:]),
           waits=cp_xk)

    cp_kact = []
    cp_kact_ntd = [None] * NNT
    stageH = {0: [stageF[0][-1]], 1: [stageF[1][-1]]}
    slotH = {0: list(slotF[0]), 1: list(slotF[1])}
    t0_free = {0: None, 1: None}
    bank_freeH = [None] * NNT
    for dm in range(NCH):
        par = dm % 2
        slot = (KWB0, KWB1)[par]
        cp_d = em("y", lambda y, dm=dm, par=par: y.dma_start(
            out=WST[:, par * 1024:par * 1024 + 1024].rearrange(
                "p (c j) -> p c j", j=128),
            in_=wkf_h[dm].rearrange("c p j -> p c j")),
            waits=stageH[par] + slotH[par],
            tag=f"kwd{dm}", dgroup=f"kw{dm // 2}")
        cst = em("v", lambda v, slot=slot, par=par: v.tensor_copy(
            out=slot[:], in_=WST[:, par * 1024:par * 1024 + 1024]),
            waits=[cp_d], tag=f"kwc{dm}")
        stageH[par] = [cst]
        reads = []
        for nt in range(NNT):
            last = None
            for cc in range(NCH):
                w = [cst]
                if cc == 0:
                    w.append(bank_freeH[nt])
                    w.append(cp_xk_nt[nt])
                last = mm(PS[nt][:, :], slot[:, 128 * cc:128 * cc + 128],
                          chsl(XK, cc, nt), start=(cc == 0),
                          stop=(cc == NCH - 1), waits=w, tag=f"kp{dm}_{cc}_{nt}")
                reads.append(last)
            tp = nt % 2
            t0 = (T0A, T0B)[tp]
            r1 = em("s", lambda s, t0=t0, nt=nt: s.activation(
                out=t0[:], in_=PS[nt][:, :], func=AF.Relu),
                waits=[last, t0_free[tp]], tag=f"kr{dm}_{nt}")
            bank_freeH[nt] = r1
            ks = em("p", lambda g, dm=dm, nt=nt, t0=t0: g.tensor_tensor(
                out=chsl(KACT, dm, nt), in0=t0[:], in1=t0[:], op=ALU.mult),
                waits=[r1], tag=f"ks{dm}_{nt}")
            t0_free[tp] = ks
            cp_kact.append(ks)
            cp_kact_ntd[nt] = ks
        slotH[par] = reads
    if "KACT" in debug:
        em("g", lambda g: g.dma_start(out=dbg_h["KACT"][:], in_=KACT[:]),
           waits=cp_kact)

    cp_xr, cp_xr_nt = mix(XR, MURC, cp_mur, cp_kact_ntd, "mr")
    if "XR" in debug:
        em("g", lambda g: g.dma_start(out=dbg_h["XR"][:], in_=XR[:]),
           waits=cp_xr)

    cp_r2 = []
    wst_r = [stageH[0][-1], stageH[1][-1]]
    wst_v = [stageH[0][-1], stageH[1][-1]]
    rwb_readers = []
    vwb_readers = []
    t0a_free = None
    t0b_free = None
    bank4 = {0: None, 1: None, 2: None, 3: None}
    for dm in range(NCH):
        cp_dr = em("y", lambda y, dm=dm: y.dma_start(
            out=WST[:, 0:1024].rearrange("p (c j) -> p c j", j=128),
            in_=wr_h[dm].rearrange("c p j -> p c j")), waits=wst_r + rwb_readers, tag=f"rwd{dm}", dgroup=f"h4w{dm}")
        cr = em("s", lambda s: s.copy(out=RWB[:], in_=WST[:, 0:1024]),
                waits=[cp_dr], tag=f"rwc{dm}")
        wst_r = [cr]
        cp_dv = em("y", lambda y, dm=dm: y.dma_start(
            out=WST[:, 1024:2048].rearrange("p (c j) -> p c j", j=128),
            in_=wvf_h[dm].rearrange("c p j -> p c j")), waits=wst_v + vwb_readers, tag=f"vwd{dm}", dgroup=f"h4w{dm}")
        cv = em("v", lambda v: v.tensor_copy(out=VWB[:], in_=WST[:, 1024:2048]),
                waits=[cp_dv], tag=f"vwc{dm}")
        wst_v = [cv]
        rwb_readers = []
        vwb_readers = []
        for nt in range(NNT):
            b = nt % 2
            lastr = lastv = None
            for cc in range(NCH):
                w1 = [cr]
                w2 = [cv]
                if cc == 0:
                    w1.append(bank4[b])
                    w2.append(bank4[2 + b])
                    w1.append(cp_xr_nt[nt])
                lastr = mm(PS[b][:, :], RWB[:, 128 * cc:128 * cc + 128],
                           chsl(XR, cc, nt), start=(cc == 0),
                           stop=(cc == NCH - 1), waits=w1,
                           tag=f"rp{dm}_{cc}_{nt}")
                lastv = mm(PS[2 + b][:, :], VWB[:, 128 * cc:128 * cc + 128],
                           chsl(KACT, cc, nt), start=(cc == 0),
                           stop=(cc == NCH - 1), waits=w2,
                           tag=f"v2{dm}_{cc}_{nt}")
                rwb_readers.append(lastr)
                vwb_readers.append(lastv)
            sg = em("s", lambda s, b=b: s.activation(
                out=T0A[:], in_=PS[b][:, :], func=AF.Sigmoid),
                waits=[lastr, t0a_free], tag=f"sg{dm}_{nt}")
            bank4[b] = sg
            g1 = em("v", lambda v, b=b: v.tensor_tensor(
                out=T0B[:], in0=PS[2 + b][:, :], in1=T0A[:], op=ALU.mult),
                waits=[sg, lastv, t0b_free], tag=f"g1{dm}_{nt}")
            t0a_free = g1
            bank4[2 + b] = g1
            rr = em("v", lambda v, dm=dm, nt=nt: v.tensor_tensor(
                out=chsl(XT2, dm, nt), in0=T0B[:], in1=chsl(XT2, dm, nt),
                op=ALU.add), waits=[g1], tag=f"r2_{dm}_{nt}")
            t0b_free = rr
            cp_r2.append(rr)
    if upto == "H":
        pass

    cp_out, _, cp_out_ch = layernorm_inplace(cp_r2[-1], MU2, RSTD2, "gb")
    for ch in range(NCH):
        em("g", lambda g, ch=ch: g.dma_start(
            out=out_h[128 * ch:128 * ch + 128, :], in_=chsl(XT2, ch)),
            waits=[cp_out_ch[ch]], tag=f"odma{ch}")
    return nc, p, dbg_h


def finalize(nc, p):
    from contextlib import ExitStack
    refs = set()
    for e in p.ops:
        for fn, ws, key, c, tag, dg in p.ops[e]:
            for we, wv in ws:
                refs.add((we, wv))
    keymap = {}
    misc_total = 0
    for q in ("y", "g"):
        open_group = {}
        for op in p.ops[q]:
            if len(op) < 6:
                continue
            key, dg = op[2], op[5]
            if key in ("p",):
                continue
            if (key, 16) not in refs:
                keymap[key] = "dmisc"
                misc_total += 16
                op[3] = misc_total
            elif dg is not None and dg in open_group:
                lead = open_group.pop(dg)
                keymap[key] = lead
                op[3] = 32
            else:
                keymap[key] = key
                if dg is not None:
                    open_group[dg] = key
    eng_refs = {k: set() for k in ("t", "v", "s", "p")}
    for we, wv in refs:
        if we in eng_refs:
            eng_refs[we].add(wv)
    remap = {k: {v: i + 1 for i, v in enumerate(sorted(eng_refs[k]))}
             for k in eng_refs}
    dma_keys = sorted({keymap[op[2]] for q in ("y", "g") for op in p.ops[q]
                       if op[2] != "p"})
    finals = {}
    for q in ("y", "g"):
        for op in p.ops[q]:
            if op[2] == "p":
                continue
            k = keymap[op[2]]
            finals[k] = max(finals.get(k, 0), op[3])
    with ExitStack() as st:
        sems = {k: st.enter_context(nc.semaphore(f"sem_{k}"))
                for k in ["t", "v", "s", "p"] + dma_keys}
        block = st.enter_context(nc.Block())

        def rwait(eng, hw, we, wv):
            k = keymap.get(we, we)
            if we in remap:
                rv = remap[we][wv]
            else:
                rv = finals[k]
            if hw.get(k, 0) < rv:
                eng.wait_ge(sems[k], rv)
                hw[k] = rv

        def make_runner(e):
            ops = p.ops[e]

            def run(eng):
                hw = {}
                for fn, ws, key, c, tag, dg in ops:
                    for we, wv in ws:
                        rwait(eng, hw, we, wv)
                    ins = fn(eng)
                    if key in ("t", "v", "s", "p"):
                        if c in remap[key] and ins is not None:
                            ins.then_inc(sems[key], 1)
                    else:
                        ins.then_inc(sems[keymap[key]], 16)
                if e == "g":
                    for k, v in sorted(finals.items()):
                        if hw.get(k, 0) < v:
                            eng.wait_ge(sems[k], v)
            return run

        block.tensor(make_runner("t"))
        block.vector(make_runner("v"))
        block.scalar(make_runner("s"))
        block.sync(make_runner("y"))
        block.gpsimd(make_runner("g"))
    return nc


def host_inputs(full):
    x = np.ascontiguousarray(full["L0_attn_x"], dtype=np.float32)
    Wq = np.ascontiguousarray(full["L0_attn_Wq"], dtype=np.float32)
    Wkv = np.ascontiguousarray(full["L0_attn_Wkv"], dtype=np.float32)
    Wb = np.ascontiguousarray(full["L0_attn_W_bilinear"], dtype=np.float32)
    Wo = np.ascontiguousarray(full["L0_attn_Wo"], dtype=np.float32)
    Wr = np.ascontiguousarray(full["L0_ffn_W_r"], dtype=np.float32)
    Wkf = np.ascontiguousarray(full["L0_ffn_W_k"], dtype=np.float32)
    Wvf = np.ascontiguousarray(full["L0_ffn_W_v"], dtype=np.float32)
    gam = np.ascontiguousarray(full["L0_attn_gamma"], dtype=np.float32)
    mur = np.ascontiguousarray(full["L0_ffn_mu_r"], dtype=np.float32)
    muk = np.ascontiguousarray(full["L0_ffn_mu_k"], dtype=np.float32)

    def dm_tiles(W):
        return np.ascontiguousarray(
            W.reshape(8, 128, 8, 128).transpose(2, 0, 1, 3))

    cos2, sin2 = _rope_tables()
    shared = {
        "wq": dm_tiles(Wq),
        "wk": dm_tiles(Wkv[:, 0:C]),
        "wv": np.ascontiguousarray(Wkv[:, C:2 * C].reshape(8, 128, 1024)),
        "wo": dm_tiles(Wo),
        "wb": np.ascontiguousarray(Wb.reshape(8, 128, 64)),
        "wr": dm_tiles(np.ascontiguousarray(Wr.T)),
        "wkf": dm_tiles(np.ascontiguousarray(Wkf.T)),
        "wvf": dm_tiles(np.ascontiguousarray(Wvf.T)),
        "gam": np.ascontiguousarray(gam.reshape(8, 128).T),
        "mur": np.ascontiguousarray(mur.reshape(8, 128).T),
        "muk": np.ascontiguousarray(muk.reshape(8, 128).T),
        "cos2": cos2,
        "sin2": sin2,
        "r2t": _perm_r2t(),
        "epat": _epat(),
        "id8": np.eye(8, dtype=np.float32),
        "onesr": np.ones((1, 128), dtype=np.float32),
    }
    in_maps = []
    for c in range(8):
        m = dict(shared)
        m["xt"] = np.ascontiguousarray(x[c].T)
        in_maps.append(m)
    return in_maps


_NC_CACHE = {}


def kernel(**inputs):
    if "main" not in _NC_CACHE:
        nc, p, dbg = build_nc()
        _NC_CACHE["main"] = finalize(nc, p)
    nc = _NC_CACHE["main"]
    in_maps = host_inputs(inputs)
    res = run_bass_kernel_spmd(nc, in_maps, core_ids=list(range(8)))
    outs = [np.ascontiguousarray(r["out"].T) for r in res.results]
    return np.stack(outs, axis=0)


# version 14
# speedup vs baseline: 1.0250x; 1.0250x over previous
import numpy as np
import ml_dtypes
import concourse.bass as bass
import concourse.mybir as mybir
from concourse.bass_utils import run_bass_kernel_spmd

F32 = mybir.dt.float32
BF16 = mybir.dt.bfloat16
AF = mybir.ActivationFunctionType
ALU = mybir.AluOpType
AX = mybir.AxisListType.X
f32 = np.float32

T = 2048
C = 1024
DH = 64
NCH = 8
NNT = 4
NTC = 16
NPR = 8
EPS = 1e-5

class Prog:

    def __init__(self):
        self.ops = {e: [] for e in ("t", "v", "s", "y", "g")}
        self.cnt = {e: 0 for e in ("t", "v", "s", "p")}
        self.nd = 0

    def em(self, e, fn, waits=(), tag="", dgroup=None):
        ws = []
        for w in waits:
            if w is None:
                continue
            we, wv = w
            if we == e and e != "p" and wv < self.cnt[e] - 1:
                continue
            if we in ("t", "v", "s"):
                assert wv <= self.cnt[we], f"future {we}:{wv} ({tag})"
            ws.append((we, wv))
        if e in ("y", "g"):
            key = f"d{self.nd}"
            self.nd += 1
            self.ops[e].append([fn, ws, key, 16, tag, dgroup])
            tok = (key, 16)
        elif e == "p":
            self.ops["g"].append([fn, ws, "p", self.cnt["p"] + 1, tag, None])
            self.cnt["p"] += 1
            tok = ("p", self.cnt["p"])
        else:
            self.ops[e].append([fn, ws, e, self.cnt[e] + 1, tag, None])
            self.cnt[e] += 1
            tok = (e, self.cnt[e])
        return tok


def _rope_tables():
    theta = 1.0 / (10000.0 ** (np.arange(0, DH, 2, dtype=np.float64) / DH))
    pos = np.arange(T, dtype=np.float64)
    freqs = pos[:, None] * theta[None, :]
    emb = np.concatenate([freqs, freqs], axis=-1)
    cosT = np.cos(emb).T.astype(np.float32)
    sinT = np.sin(emb).T.astype(np.float32)
    return (np.ascontiguousarray(np.concatenate([cosT, cosT], axis=0)),
            np.ascontiguousarray(np.concatenate([sinT, sinT], axis=0)))


def _perm_r2t():
    R = np.zeros((DH, DH), dtype=np.float32)
    for m in range(DH // 2):
        R[2 * m, 2 * m + 1] = -1.0
        R[2 * m + 1, 2 * m] = 1.0
    R2 = np.zeros((128, 128), dtype=np.float32)
    R2[0:64, 0:64] = R
    R2[64:128, 64:128] = R
    return np.ascontiguousarray(R2.T)


def _epat():
    E = np.zeros((128, 8 * NPR), dtype=np.float32)
    for ch in range(NPR):
        E[:, 8 * ch + ch] = 1.0
    return E


def build_nc(debug=(), upto="ALL", canary=True):
    nc = bass.Bass("TRN2", target_bir_lowering=False)
    p = Prog()
    em = p.em

    xt_h = nc.dram_tensor("xt", [C, T], F32, kind="ExternalInput")
    out_h = nc.dram_tensor("out", [C, T], F32, kind="ExternalOutput")
    wq_h = nc.dram_tensor("wq", [8, 8, 128, 128], F32, kind="ExternalInput")
    wk_h = nc.dram_tensor("wk", [8, 8, 128, 128], F32, kind="ExternalInput")
    wv_h = nc.dram_tensor("wv", [8, 128, 1024], F32, kind="ExternalInput")
    wo_h = nc.dram_tensor("wo", [8, 8, 128, 128], F32, kind="ExternalInput")
    wb_h = nc.dram_tensor("wb", [8, 128, 64], F32, kind="ExternalInput")
    wr_h = nc.dram_tensor("wr", [8, 8, 128, 128], F32, kind="ExternalInput")
    wkf_h = nc.dram_tensor("wkf", [8, 8, 128, 128], F32, kind="ExternalInput")
    wvf_h = nc.dram_tensor("wvf", [8, 8, 128, 128], F32, kind="ExternalInput")
    gam_h = nc.dram_tensor("gam", [128, 8], F32, kind="ExternalInput")
    mur_h = nc.dram_tensor("mur", [128, 8], F32, kind="ExternalInput")
    muk_h = nc.dram_tensor("muk", [128, 8], F32, kind="ExternalInput")
    cos_h = nc.dram_tensor("cos2", [128, T], F32, kind="ExternalInput")
    sin_h = nc.dram_tensor("sin2", [128, T], F32, kind="ExternalInput")
    r2t_h = nc.dram_tensor("r2t", [128, 128], F32, kind="ExternalInput")
    epat_h = nc.dram_tensor("epat", [128, 64], F32, kind="ExternalInput")
    id8_h = nc.dram_tensor("id8", [8, 8], F32, kind="ExternalInput")
    onesr_h = nc.dram_tensor("onesr", [1, 128], F32, kind="ExternalInput")
    spill_h = nc.dram_tensor("xspill", [C, T], F32)

    dbg_h = {}
    for name, shape, dt in (
        ("XBF", [128, 16384], BF16), ("QT", [128, 16384], BF16),
        ("KTT", [128, 16384], BF16), ("VT", [128, 16640], BF16),
        ("AOT", [128, 16384], BF16), ("R1", [128, 16384], F32),
        ("Y", [128, 16384], F32), ("XK", [128, 16384], BF16),
        ("KACT", [128, 16384], BF16), ("XR", [128, 16384], BF16),
        ("NMU", [1, 2048], F32), ("RSTD", [1, 2048], F32),
        ("CVEC", [128, 8], F32), ("QPRE6", [128, 2048], BF16),
        ("WBB", [128, 512], BF16),
        ("WQS6", [128, 1024], BF16), ("WST6", [128, 2048], F32),
    ):
        if name in debug:
            dbg_h[name] = nc.dram_tensor("dbg_" + name, shape, dt,
                                         kind="ExternalOutput")

    A0 = nc.alloc_sbuf_tensor_at
    SB0 = 16512
    def A(name, shape, dt, offset):
        return A0(name, shape, dt, offset=SB0 + offset)
    XT = A("XT", [128, 16384], F32, offset=0)
    QT = A("QT", [128, 16384], BF16, offset=0)
    KTT = A("KTT", [128, 16384], BF16, offset=32768)
    XT2 = A("XT2", [128, 16384], F32, offset=0)
    XBF = A("XBF", [128, 16384], BF16, offset=65536)
    AOT = A("AOT", [128, 16384], BF16, offset=65536)
    XR = A("XR", [128, 16384], BF16, offset=65536)
    VT = A("VT", [128, 16640], BF16, offset=98304)
    XK = A("XK", [128, 16384], BF16, offset=98304)
    WVB = A("WVB", [128, 8192], BF16, offset=131584)
    KACT = A("KACT", [128, 16384], BF16, offset=131584)
    BOUNCE = A("BOUNCE", [1, 1024], F32, offset=131584)
    CPROW = A("CPROW", [1, 16], F32, offset=135616)
    NMU = A("NMU", [1, 2048], F32, offset=147968)
    NCKV = A("NCKV", [1, 2048], F32, offset=156160)
    WST = A("WST", [128, 2048], F32, offset=164352)
    COS = A("COS", [128, 2048], F32, offset=172544)
    SIN = A("SIN", [128, 2048], F32, offset=180736)
    MU2 = A("MU2", [1, 2048], F32, offset=172544)
    RSTD2 = A("RSTD2", [1, 2048], F32, offset=180736)
    WQS0 = A("WQS0", [128, 1024], BF16, offset=188928)
    WQS1 = A("WQS1", [128, 1024], BF16, offset=190976)
    WKS0 = A("WKS0", [128, 1024], BF16, offset=193024)
    WKS1 = A("WKS1", [128, 1024], BF16, offset=195072)
    KWB0, KWB1 = WQS0, WQS1
    RWB, VWB = WKS0, WKS1
    QPRE = A("QPRE", [128, 2048], BF16, offset=197120)
    QPRE2 = A("QPRE2", [128, 2048], BF16, offset=98304)
    MIXT = A("MIXT", [128, 512], F32, offset=197120)
    T0A = A("T0A", [128, 512], F32, offset=201216)
    T0B = A("T0B", [128, 512], F32, offset=203264)
    T0W = A("T0W", [128, 1024], F32, offset=201216)
    PBIG = [A(f"PBIG{i}", [128, 1024], BF16, offset=205312 + 2048 * i)
            for i in range(2)]
    PBIG += [A(f"PBIG{i + 2}", [128, 1024], BF16, offset=135680 + 2048 * i)
             for i in range(2)]
    PBUF = [PBIG[i // 2][:, 512 * (i % 2):512 * (i % 2) + 512]
            for i in range(8)]
    GAMC = A("GAMC", [128, 8], F32, offset=209408)
    MURC = A("MURC", [128, 8], F32, offset=209440)
    MUKC = A("MUKC", [128, 8], F32, offset=209472)
    RSTDC = A("RSTDC", [128, 16], F32, offset=209504)
    CVEC = A("CVEC", [128, 8], F32, offset=209568)
    CP8 = A("CP8", [8, 2], F32, offset=209600)
    Q2M = A("Q2M", [8, 8], F32, offset=209632)
    Z2M = A("Z2M", [8, 8], F32, offset=209664)
    ONESR = A("ONESR", [1, 128], F32, offset=209696)
    ONESCB = A("ONESCB", [128, 1], BF16, offset=210208)
    ONESCF = A("ONESCF", [128, 1], F32, offset=211712)
    ID8 = A("ID8", [8, 8], F32, offset=210240)
    R2TB = A("R2TB", [128, 128], BF16, offset=210272)
    EPATB = A("EPATB", [128, 64], BF16, offset=210528)
    WBB = A("WBB", [128, 512], BF16, offset=210656)
    MUKC2 = A("MUKC2", [128, 8], F32, offset=211744)
    MURC2 = A("MURC2", [128, 8], F32, offset=211776)

    PSA = nc.alloc_psum_tensor("psA", [128, 1024], F32)
    PSB = nc.alloc_psum_tensor("psB", [128, 1024], F32)
    PSC = nc.alloc_psum_tensor("psC", [128, 1024], F32)
    PSD = nc.alloc_psum_tensor("psD", [128, 1024], F32)
    PS = [PSA[:, 0:512], PSA[:, 512:1024], PSB[:, 0:512], PSB[:, 512:1024],
          PSC[:, 0:512], PSC[:, 512:1024], PSD[:, 0:512], PSD[:, 512:1024]]
    PSAB = [PSA, PSB]

    def chsl(buf, ch, nt=None):
        if nt is None:
            return buf[:, 2048 * ch: 2048 * ch + 2048]
        return buf[:, 2048 * ch + 512 * nt: 2048 * ch + 512 * nt + 512]

    def mm(psap, lhsT, rhs, start, stop, waits=(), tp=None, tag=""):
        return em("t", lambda t, a=psap, l=lhsT, r=rhs, s0=start, s1=stop, tpp=tp:
                  t.matmul(a, l, r, start=s0, stop=s1, tile_position=tpp),
                  waits=waits, tag=tag)

    cp_cos = em("g", lambda g: g.dma_start(out=COS[:], in_=cos_h[:]), dgroup="tbl")
    cp_sin = em("g", lambda g: g.dma_start(out=SIN[:], in_=sin_h[:]), dgroup="tbl")
    cp_gam = em("y", lambda y: y.dma_start(out=GAMC[:], in_=gam_h[:]), dgroup="c0a")
    cp_mur = em("y", lambda y: y.dma_start(out=MURC[:], in_=mur_h[:]), dgroup="c0a")
    cp_muk = em("y", lambda y: y.dma_start(out=MUKC[:], in_=muk_h[:]), dgroup="c0b")
    cp_muk2 = em("s", lambda s: s.activation(
        out=MUKC2[:], in_=MUKC[:], func=AF.Copy, scale=f32(-1.0),
        bias=1.0), waits=[cp_muk], tag="muk2")
    cp_mur2 = em("s", lambda s: s.activation(
        out=MURC2[:], in_=MURC[:], func=AF.Copy, scale=f32(-1.0),
        bias=1.0), waits=[cp_mur], tag="mur2")
    cp_id8 = em("y", lambda y: y.dma_start(out=ID8[:], in_=id8_h[:]), dgroup="c0b")
    cp_onr = em("y", lambda y: y.dma_start(out=ONESR[:], in_=onesr_h[:]))
    cp_r2ts = em("y", lambda y: y.dma_start(out=WST[:, 0:128], in_=r2t_h[:]), dgroup="c0c")
    cp_epts = em("y", lambda y: y.dma_start(out=WST[:, 128:192], in_=epat_h[:]), dgroup="c0c")
    cp_r2tb = em("v", lambda v: v.tensor_copy(out=R2TB[:], in_=WST[:, 0:128]),
                 waits=[cp_r2ts])
    cp_eptb = em("v", lambda v: v.tensor_copy(out=EPATB[:], in_=WST[:, 128:192]),
                 waits=[cp_epts])
    cp_onescb = em("v", lambda v: v.memset(ONESCB[:], 1.0))
    cp_onescf = em("v", lambda v: v.memset(ONESCF[:], 1.0))
    cp_can = None
    if canary:
        em("v", lambda v: v.memset(QPRE[:], 0.0))
        em("v", lambda v: v.memset(QPRE2[:], 0.0))
        for _pb in PBIG:
            em("v", lambda v, _pb=_pb: v.memset(_pb[:], 0.0))
        em("v", lambda v: v.memset(T0A[:], 0.0))
        cp_can = em("v", lambda v: v.memset(T0B[:], 0.0))
    cp_wbs = []
    for pr in range(NPR):
        cpw = em("y", lambda y, pr=pr: y.dma_start(
            out=WST[:, 192 + 64 * pr: 256 + 64 * pr], in_=wb_h[pr]),
            dgroup=f"wb{pr // 2}")
        cp_wbs.append(em("v", lambda v, pr=pr: v.tensor_copy(
            out=WBB[:, 64 * pr: 64 * pr + 64],
            in_=WST[:, 192 + 64 * pr: 256 + 64 * pr]), waits=[cpw]))
    cp_wst_free = [cp_r2tb, cp_eptb] + cp_wbs

    cp_x = [em("g", lambda g, ch=ch: g.dma_start(
        out=chsl(XT, ch), in_=xt_h[128 * ch:128 * ch + 128, :]),
        tag=f"xin{ch}", dgroup=f"xin{ch // 2}")
        for ch in range(NCH)]

    cp_sq2 = [[None] * NNT for _ in range(NCH)]
    cp_sx = [[None] * NNT for _ in range(NCH)]
    for ch in range(NCH):
        for nt in range(NNT):
            w = [cp_x[ch]]
            if ch > 0:
                w.append(cp_sq2[ch - 1][nt])
            sq = em("s", lambda s, ch=ch, nt=nt: s.square(
                out=PBUF[nt][:], in_=chsl(XT, ch, nt)),
                waits=w + ([cp_can] if ch == 0 else []),
                tag=f"asq_{ch}{nt}")
            cp_sx[ch][nt] = mm(PS[nt][0:1, :], ONESCF[:, 0:1],
                               chsl(XT, ch, nt),
                               start=(ch == 0), stop=(ch == NCH - 1),
                               waits=[cp_x[ch], cp_onescf], tag=f"asx{ch}{nt}")
            cp_sq2[ch][nt] = mm(PS[4 + nt][0:1, :], ONESCB[:, 0:1], PBUF[nt][:],
                                start=(ch == 0), stop=(ch == NCH - 1),
                                waits=[sq, cp_onescb], tag=f"asq{ch}{nt}")

    cp_nmu = [None] * NNT
    cp_rstd = [None] * NNT
    cp_c3 = [None] * NNT
    for nt in range(NNT):
        wstat = [cp_sq2[NCH - 1][nt]]
        cp_nmu[nt] = em("s", lambda s, nt=nt: s.mul(
            out=NMU[0:1, 512 * nt:512 * nt + 512], in_=PS[nt][0:1, :],
            mul=f32(-1.0 / C)), waits=wstat, tag=f"nmu{nt}")
        c1 = em("v", lambda v, nt=nt: v.tensor_tensor(
            out=T0A[0:1, :], in0=NMU[0:1, 512 * nt:512 * nt + 512],
            in1=NMU[0:1, 512 * nt:512 * nt + 512], op=ALU.mult),
            waits=[cp_nmu[nt]], tag=f"bmu2{nt}")
        c2 = em("s", lambda s, nt=nt: s.mul(
            out=T0B[0:1, :], in_=PS[4 + nt][0:1, :], mul=f32(1.0 / C)),
            waits=wstat + [cp_rstd[nt - 1] if nt else cp_can], tag=f"bmsq{nt}")
        cp_c3[nt] = em("v", lambda v: v.tensor_tensor(
            out=T0B[0:1, :], in0=T0B[0:1, :], in1=T0A[0:1, :],
            op=ALU.subtract), waits=[c1, c2], tag=f"bvar{nt}")
        c4 = em("s", lambda s: s.activation(
            out=T0B[0:1, :], in_=T0B[0:1, :], func=AF.Sqrt, bias=f32(EPS)),
            waits=[cp_c3[nt]], tag=f"bsd{nt}")
        cp_rstd[nt] = em("v", lambda v, nt=nt: v.reciprocal(
            out=NCKV[0:1, 512 * nt:512 * nt + 512], in_=T0B[0:1, :]),
            waits=[c4], tag=f"brstd{nt}")
    cp_fold_sin = [None] * NNT
    cp_fold = []
    cp_xbf = []
    cp_spill = [None] * NCH
    cp_mub = [None] * NNT
    for nt in range(NNT):
        rb = mm(PS[nt][:, :], ONESR[0:1, 0:128],
                NCKV[0:1, 512 * nt:512 * nt + 512], start=True, stop=True,
                waits=[cp_rstd[nt], cp_onr, cp_nmu[nt]], tag=f"brb{nt}")
        fc = em("v", lambda v, nt=nt: v.tensor_tensor(
            out=COS[:, 512 * nt:512 * nt + 512],
            in0=COS[:, 512 * nt:512 * nt + 512], in1=PS[nt][:, :],
            op=ALU.mult), waits=[rb, cp_cos], tag=f"bfc{nt}")
        fs = em("v", lambda v, nt=nt: v.tensor_tensor(
            out=SIN[:, 512 * nt:512 * nt + 512],
            in0=SIN[:, 512 * nt:512 * nt + 512], in1=PS[nt][:, :],
            op=ALU.mult), waits=[rb, cp_sin], tag=f"bfs{nt}")
        cp_fold_sin[nt] = fs
        cp_fold += [fc, fs]
        cp_mub[nt] = mm(PS[4 + nt][:, :], ONESR[0:1, 0:128],
                        NMU[0:1, 512 * nt:512 * nt + 512], start=True,
                        stop=True,
                        waits=[cp_nmu[nt], cp_rstd[nt], cp_onr],
                        tag=f"bmub{nt}")
    xbf_by_ch = [[None] * NNT for _ in range(NCH)]
    for ch in range(NCH):
        for nt in range(NNT):
            t = em("v", lambda v, ch=ch, nt=nt: v.tensor_tensor(
                out=chsl(XBF, ch, nt), in0=chsl(XT, ch, nt),
                in1=PS[4 + nt][:, :], op=ALU.add), waits=[cp_mub[nt]],
                tag=f"xbf{ch}_{nt}")
            xbf_by_ch[ch][nt] = t
            cp_xbf.append(t)
        cp_spill[ch] = em("g", lambda g, ch=ch: g.dma_start(
            out=spill_h[128 * ch:128 * ch + 128, :], in_=chsl(XT, ch)),
            waits=xbf_by_ch[ch], tag=f"spill{ch}", dgroup=f"sp{ch // 2}")
    rc_last = None
    for tc in range(NTC):
        rc_last = mm(PS[0][:, tc:tc + 1], NCKV[0:1, 128 * tc:128 * tc + 128],
                     ONESR[0:1, 0:1], start=True, stop=True,
                     waits=cp_rstd + cp_fold, tag=f"brc{tc}")
    cp_rstdc = em("s", lambda s: s.copy(out=RSTDC[:], in_=PS[0][:, 0:16]),
                  waits=[rc_last], tag="brcev")

    if "NMU" in debug:
        em("g", lambda g: g.dma_start(out=dbg_h["NMU"][:], in_=NMU[0:1, :]),
           waits=cp_nmu)
    if "RSTD" in debug:
        em("g", lambda g: g.dma_start(out=dbg_h["RSTD"][:], in_=NCKV[0:1, :]),
           waits=cp_rstd + [cp_rstdc])
    if "XBF" in debug:
        em("g", lambda g: g.dma_start(out=dbg_h["XBF"][:], in_=XBF[:]),
           waits=cp_xbf)
    if upto == "B":
        return nc, p, dbg_h

    state = {
        "bank_free": {nt: (cp_rstdc if nt == 0 else cp_fold_sin[nt])
                      for nt in range(NNT)},
        "rot_free": {6: cp_xbf[-2], 7: cp_xbf[-1]},
        "qpre_readers": [[[] for _ in range(NNT)] for _ in range(2)],
        "bil_free": {0: None, 1: None},
        "t0_reader": None,
    }

    stage_state = {0: [], 1: []}

    def proj_qk(which, out_tokens):
        wtile_h = wq_h if which == "q" else wk_h
        slots = (WQS0, WQS1) if which == "q" else (WKS0, WKS1)
        slot_readers = {0: [], 1: []}
        stage_readers = stage_state
        for oc in range(NCH):
            par = oc % 2
            slot = slots[par]
            half = par * 1024
            stg = WST[:, half:half + 1024].rearrange("p (c j) -> p c j", j=128)
            wd = stage_readers[par] + (cp_wst_free if which == "q" and oc < 2 else [])
            cp_d = em("y", lambda y, stg=stg, oc=oc: y.dma_start(
                out=stg, in_=wtile_h[oc].rearrange("c p j -> p c j")),
                waits=wd, tag=f"{which}wd{oc}", dgroup=f"{which}w{oc // 2}")
            casts = []
            for cc in range(NCH):
                casts.append(em("v", lambda v, slot=slot, cc=cc, half=half:
                               v.tensor_scalar_mul(
                    out=slot[:, 128 * cc:128 * cc + 128],
                    in0=WST[:, half + 128 * cc: half + 128 * cc + 128],
                    scalar1=GAMC[:, cc:cc + 1]),
                    waits=[cp_d, cp_gam] + slot_readers[par],
                    tag=f"{which}wc{oc}_{cc}"))
            stage_readers[par] = [casts[-1]]
            reads = []
            last = [None] * NNT
            for cc in range(NCH):
                for nt in range(NNT):
                    w = [casts[cc]]
                    if cc == 0:
                        w.append(state["bank_free"][nt])
                    tok = mm(PS[nt][:, :], slot[:, 128 * cc:128 * cc + 128],
                             chsl(XBF, cc, nt), start=(cc == 0),
                             stop=(cc == NCH - 1),
                             waits=w, tag=f"{which}p{oc}_{cc}_{nt}")
                    last[nt] = tok
                    reads.append(tok)
            slot_readers[par] = reads
            qp = (QPRE, QPRE2)[par]
            for nt in range(NNT):
                cp_ev = em("s", lambda s, nt=nt, qp=qp: s.copy(
                    out=qp[:, 512 * nt:512 * nt + 512], in_=PS[nt][:, :]),
                    waits=[last[nt]] + state["qpre_readers"][par][nt],
                    tag=f"{which}ev{oc}_{nt}")
                state["qpre_readers"][par][nt] = []
                state["bank_free"][nt] = cp_ev
                rbk = 6 + nt % 2
                rot = mm(PS[rbk][0:128, :], R2TB[:],
                         qp[:, 512 * nt:512 * nt + 512], start=True, stop=True,
                         waits=[cp_ev, cp_r2tb, state["rot_free"][rbk]],
                         tag=f"{which}rot{oc}_{nt}")
                tt1 = em("v", lambda v, nt=nt, qp=qp: v.tensor_tensor(
                    out=T0A[:], in0=qp[:, 512 * nt:512 * nt + 512],
                    in1=COS[:, 512 * nt:512 * nt + 512], op=ALU.mult),
                    waits=[cp_ev, state["t0_reader"]] +
                          (cp_fold if oc == 0 else []),
                    tag=f"{which}t1_{oc}_{nt}")
                state["qpre_readers"][par][nt].append(tt1)
                tt2 = em("v", lambda v, rbk=rbk, nt=nt: v.tensor_tensor(
                    out=T0B[:], in0=PS[rbk][0:128, :],
                    in1=SIN[:, 512 * nt:512 * nt + 512], op=ALU.mult),
                    waits=[rot], tag=f"{which}t2_{oc}_{nt}")
                state["rot_free"][rbk] = tt2
                if which == "q":
                    tokq = em("v", lambda v, oc=oc, nt=nt: v.tensor_tensor(
                        out=chsl(QT, oc, nt), in0=T0A[:], in1=T0B[:], op=ALU.add),
                        waits=[tt1, tt2, cp_spill[oc // 2]],
                        tag=f"qsum{oc}_{nt}")
                    state["t0_reader"] = tokq
                    out_tokens.append(tokq)
                else:
                    kslot = nt % 2
                    kr = em("v", lambda v, kslot=kslot: v.tensor_tensor(
                        out=PBUF[kslot][:], in0=T0A[:], in1=T0B[:], op=ALU.add),
                        waits=[tt1, tt2, state["bil_free"][kslot]],
                        tag=f"krop{oc}_{nt}")
                    state["t0_reader"] = kr
                    b0 = mm(PS[nt][0:64, :], WBB[0:64, 64 * oc:64 * oc + 64],
                            PBUF[kslot][0:64, :], start=True, stop=True,
                            waits=[kr, cp_ev] + (cp_wbs if oc == 0 else []),
                            tp=(0, 0), tag=f"bil0_{oc}_{nt}")
                    b1 = mm(PS[nt][64:128, :], WBB[64:128, 64 * oc:64 * oc + 64],
                            PBUF[kslot][64:128, :], start=True, stop=True,
                            waits=[kr], tp=(64, 64), tag=f"bil1_{oc}_{nt}")
                    state["bil_free"][kslot] = b1
                    ktev = em("s", lambda s, oc=oc, nt=nt: s.copy(
                        out=chsl(KTT, oc, nt), in_=PS[nt][0:128, :]),
                        waits=[b0, b1, cp_spill[4 + oc // 2]],
                        tag=f"ktev{oc}_{nt}")
                    state["bank_free"][nt] = ktev
                    out_tokens.append(ktev)
            if which == "q" and oc == 0 and "QPRE6" in debug:
                em("g", lambda g: g.dma_start(out=dbg_h["QPRE6"][:], in_=QPRE[:]),
                   waits=[state["bank_free"][n] for n in range(NNT)])
            if which == "q" and oc == 0 and "WQS6" in debug:
                em("g", lambda g, slot=slot: g.dma_start(
                    out=dbg_h["WQS6"][:], in_=slot[:]), waits=slot_readers[par])
            if which == "q" and oc == 0 and "WST6" in debug:
                em("g", lambda g: g.dma_start(
                    out=dbg_h["WST6"][:], in_=WST[:]), waits=[casts[-1]])
        return out_tokens

    cp_q = proj_qk("q", [])
    cp_k = proj_qk("k", [])

    if "QT" in debug:
        em("g", lambda g: g.dma_start(out=dbg_h["QT"][:], in_=QT[:]), waits=cp_q)
    if "KTT" in debug:
        em("g", lambda g: g.dma_start(out=dbg_h["KTT"][:], in_=KTT[:]), waits=cp_k)
    if "WBB" in debug:
        em("g", lambda g: g.dma_start(out=dbg_h["WBB"][:], in_=WBB[:]),
           waits=cp_k)
    if upto == "C":
        return nc, p, dbg_h

    cp_wv_cast = []
    stage_readers = {0: list(stage_state[0]), 1: list(stage_state[1])}
    for cc in range(NCH):
        par = cc % 2
        half = par * 1024
        cp_d = em("y", lambda y, cc=cc, half=half: y.dma_start(
            out=WST[:, half:half + 1024], in_=wv_h[cc]),
            waits=stage_readers[par], tag=f"wvd{cc}", dgroup=f"wv{cc // 2}")
        cst = em("s", lambda s, cc=cc, half=half: s.mul(
            out=WVB[:, 1024 * cc:1024 * cc + 1024],
            in_=WST[:, half:half + 1024],
            mul=GAMC[:, cc:cc + 1]), waits=[cp_d], tag=f"wvc{cc}")
        stage_readers[par] = [cst]
        cp_wv_cast.append(cst)
    ones_ap = VT[:, 0:16640].rearrange(
        "p (tc h e) -> p tc h e", h=16, e=65)[:, :, :, 64:65]
    cp_vones = em("v", lambda v: v.memset(ones_ap, 1.0), waits=[cp_q[-1]],
                  tag="vones")
    cp_vt = [cp_vones]
    vev_free = {i: cp_k[-1] for i in range(4)}
    for tc in range(NTC):
        b = (tc % 2) * 2
        last = [None, None]
        for cc in range(NCH):
            for h2 in range(2):
                w = [cp_wv_cast[cc]]
                if cc == 0:
                    w.append(vev_free[b + h2])
                last[h2] = mm(PS[b + h2][:, :],
                              XBF[:, 2048 * cc + 128 * tc:2048 * cc + 128 * tc + 128],
                              WVB[:, 1024 * cc + 512 * h2:1024 * cc + 512 * h2 + 512],
                              start=(cc == 0), stop=(cc == NCH - 1), waits=w,
                              tag=f"vp{tc}_{cc}_{h2}")
        for h2 in range(2):
            dst = VT[:, 1040 * tc: 1040 * tc + 1040].rearrange(
                "p (h e) -> p h e", e=65)[:, 8 * h2:8 * h2 + 8, 0:64]
            src = PS[b + h2][:, :].rearrange("p (h e) -> p h e", e=64)
            cp_ev = em("v", lambda v, dst=dst, src=src, tc=tc, h2=h2:
                       v.tensor_scalar_mul(
                out=dst, in0=src, scalar1=RSTDC[:, tc:tc + 1]),
                waits=[last[h2], cp_rstdc, cp_vones], tag=f"vev{tc}_{h2}")
            vev_free[b + h2] = cp_ev
            cp_vt.append(cp_ev)

    if "VT" in debug:
        em("g", lambda g: g.dma_start(out=dbg_h["VT"][:], in_=VT[:]), waits=cp_vt)
    if upto == "D":
        return nc, p, dbg_h

    red_q = [None] * NNT
    red_z = [None] * NNT
    npat_free = {0: None, 1: None, 2: None, 3: None}
    cp_q2 = [None] * NNT
    cp_z2 = [None] * NNT
    for src, base_ps, cps, idx in ((QT, 0, cp_q2, 0), (KTT, 4, cp_z2, 1)):
        for pr in range(NPR):
            for nt in range(NNT):
                slot = 2 * (nt % 2) + idx
                w = [cp_q[-1], cp_k[-1], cp_eptb, npat_free[slot]]
                if pr == 0 and nt == 0 and idx == 0:
                    w += cp_vt
                sq = em("v", lambda v, src=src, pr=pr, nt=nt, slot=slot:
                        v.tensor_tensor(out=PBUF[slot][:], in0=chsl(src, pr, nt),
                                        in1=chsl(src, pr, nt), op=ALU.mult),
                        waits=w, tag=f"esq{idx}_{pr}_{nt}")
                tok = mm(PS[base_ps + nt][0:8, :], EPATB[:, 8 * pr:8 * pr + 8],
                         PBUF[slot][:], start=(pr == 0), stop=(pr == NPR - 1),
                         waits=[sq] + ([cp_vt[-1]] if pr == 0 and idx == 1 else []),
                         tag=f"ept{idx}_{pr}_{nt}")
                npat_free[slot] = tok
                cps[nt] = tok
    for nt in range(NNT):
        red_q[nt] = em("v", lambda v, nt=nt: v.reduce_max(
            out=Q2M[0:8, nt:nt + 1], in_=PS[nt][0:8, :], axis=AX),
            waits=[cp_q2[nt]], tag=f"redq{nt}")
        red_z[nt] = em("v", lambda v, nt=nt: v.reduce_max(
            out=Z2M[0:8, nt:nt + 1], in_=PS[4 + nt][0:8, :], axis=AX),
            waits=[cp_z2[nt]], tag=f"redz{nt}")
    q1 = em("v", lambda v: v.tensor_tensor(out=Q2M[0:8, 0:1], in0=Q2M[0:8, 0:1],
                                           in1=Q2M[0:8, 1:2], op=ALU.max),
            waits=[red_q[3]])
    q2 = em("v", lambda v: v.tensor_tensor(out=Q2M[0:8, 2:3], in0=Q2M[0:8, 2:3],
                                           in1=Q2M[0:8, 3:4], op=ALU.max),
            waits=[red_q[2], red_q[3]])
    q3 = em("v", lambda v: v.tensor_tensor(out=Q2M[0:8, 0:1], in0=Q2M[0:8, 0:1],
                                           in1=Q2M[0:8, 2:3], op=ALU.max),
            waits=[q1, q2])
    z1 = em("v", lambda v: v.tensor_tensor(out=Z2M[0:8, 0:1], in0=Z2M[0:8, 0:1],
                                           in1=Z2M[0:8, 1:2], op=ALU.max),
            waits=[red_z[3]])
    z2 = em("v", lambda v: v.tensor_tensor(out=Z2M[0:8, 2:3], in0=Z2M[0:8, 2:3],
                                           in1=Z2M[0:8, 3:4], op=ALU.max),
            waits=[red_z[2], red_z[3]])
    z3 = em("v", lambda v: v.tensor_tensor(out=Z2M[0:8, 0:1], in0=Z2M[0:8, 0:1],
                                           in1=Z2M[0:8, 2:3], op=ALU.max),
            waits=[z1, z2])
    e1 = em("v", lambda v: v.tensor_tensor(out=CP8[0:8, 0:1], in0=Q2M[0:8, 0:1],
                                           in1=Z2M[0:8, 0:1], op=ALU.mult),
            waits=[q3, z3], tag="cpmul")
    e2 = em("s", lambda s: s.activation(out=CP8[0:8, 0:1], in_=CP8[0:8, 0:1],
                                        func=AF.Sqrt, scale=f32(1.0 / 64.0)),
            waits=[e1], tag="cpsqrt")
    tr = mm(PS[0][0:1, 0:8], CP8[0:8, 0:1], ID8[:], start=True, stop=True,
            waits=[e2, cp_id8], tag="cptr")
    e3 = em("s", lambda s: s.copy(out=CPROW[0:1, 0:8], in_=PS[0][0:1, 0:8]),
            waits=[tr], tag="cprow")
    br = mm(PS[1][:, 0:8], ONESR[0:1, 0:128], CPROW[0:1, 0:8], start=True,
            stop=True, waits=[e3], tag="cpbr")
    cp_cvec = em("s", lambda s: s.mul(out=CVEC[:], in_=PS[1][:, 0:8],
                                      mul=f32(-1.0)), waits=[br], tag="cvec")
    if "CVEC" in debug:
        em("g", lambda g: g.dma_start(out=dbg_h["CVEC"][:], in_=CVEC[:]),
           waits=[cp_cvec])

    PSAB3 = [PSA, PSB, PSD]
    expdone = {0: cp_cvec, 1: red_q[3], 2: red_z[3]}
    pvprev = {0: None, 1: None, 2: None, 3: None}
    o_free = {0: red_z[0], 1: red_z[1]}
    bounce_free = red_z[2]
    cp_aot = []
    cp_qk_last = None
    for pr in range(NPR):
        for qt in range(NNT):
            exptok = {}
            pv_last = [None, None]

            def emit_pv(kc, hh, pr=pr, qt=qt, pv_last=pv_last, exptok=exptok):
                wpv = [exptok[kc]]
                if kc == 0:
                    wpv.append(o_free[hh])
                pv = mm(PSC[0:65, 512 * hh:512 * hh + 512],
                        VT[:, 1040 * kc + 65 * (2 * pr + hh):
                           1040 * kc + 65 * (2 * pr + hh) + 65],
                        PBUF[2 * (kc % 4) + hh][:], start=(kc == 0),
                        stop=(kc == NTC - 1),
                        waits=wpv, tag=f"pv{pr}_{qt}_{kc}_{hh}")
                pv_last[hh] = pv
                return pv

            for kc in range(NTC):
                par = kc % 3
                slot4 = kc % 4
                qks = []
                for hh in range(2):
                    base = 64 * hh
                    qk = mm(PSAB3[par][:, 512 * hh:512 * hh + 512],
                            KTT[base:base + 64,
                                2048 * pr + 128 * kc:2048 * pr + 128 * kc + 128],
                            QT[base:base + 64,
                               2048 * pr + 512 * qt:2048 * pr + 512 * qt + 512],
                            start=True, stop=True,
                            waits=[expdone[par]],
                            tag=f"qk{pr}_{qt}_{kc}_{hh}")
                    qks.append(qk)
                    cp_qk_last = qk
                pvw = [pvprev[slot4]]
                if kc >= 4:
                    pvw = [emit_pv(kc - 4, 0), emit_pv(kc - 4, 1)]
                ex = em("s", lambda s, slot4=slot4, par=par, pr=pr: s.activation(
                    out=PBIG[slot4][:], in_=PSAB3[par][:, :], func=AF.Exp,
                    scale=f32(0.125), bias=CVEC[:, pr:pr + 1]),
                    waits=qks + [cp_cvec] + pvw,
                    tag=f"ex{pr}_{qt}_{kc}")
                exptok[kc] = ex
                expdone[par] = ex
            for kc in (NTC - 4, NTC - 3, NTC - 2, NTC - 1):
                emit_pv(kc, 0)
                pvprev[kc % 4] = emit_pv(kc, 1)
            rl = em("s", lambda s: s.activation(
                out=BOUNCE[0:1, :], in_=PSC[64:65, 0:1024], func=AF.Ln),
                waits=[pv_last[0], pv_last[1], bounce_free],
                tag=f"rl{pr}_{qt}")
            rc = em("s", lambda s: s.activation(
                out=BOUNCE[0:1, :], in_=BOUNCE[0:1, :], func=AF.Exp,
                scale=f32(-1.0)), waits=[rl], tag=f"rc{pr}_{qt}")
            nbr0 = mm(PSD[0:64, 0:512], ONESR[0:1, 0:64], BOUNCE[0:1, 0:512],
                      start=True, stop=True, waits=[rc, expdone[2]],
                      tag=f"nbr0{pr}_{qt}")
            nbr1 = mm(PSD[0:64, 512:1024], ONESR[0:1, 0:64],
                      BOUNCE[0:1, 512:1024], start=True, stop=True,
                      waits=[rc], tag=f"nbr1{pr}_{qt}")
            cpc = em("v", lambda v: v.tensor_copy(
                out=T0W[0:64, :], in_=PSD[0:64, 0:1024]),
                waits=[nbr0, nbr1, o_free[0]], tag=f"ncp{pr}_{qt}")
            bounce_free = cpc
            expdone[2] = cpc
            for hh in range(2):
                tt = em("v", lambda v, hh=hh, pr=pr, qt=qt: v.tensor_tensor(
                    out=AOT[64 * hh:64 * hh + 64,
                            2048 * pr + 512 * qt:2048 * pr + 512 * qt + 512],
                    in0=PSC[0:64, 512 * hh:512 * hh + 512],
                    in1=T0W[0:64, 512 * hh:512 * hh + 512], op=ALU.mult),
                    waits=[cpc], tag=f"ntt{pr}_{qt}_{hh}")
                o_free[hh] = tt
                cp_aot.append(tt)

    if "AOT" in debug:
        em("g", lambda g: g.dma_start(out=dbg_h["AOT"][:], in_=AOT[:]),
           waits=cp_aot)
    if upto == "E":
        return nc, p, dbg_h

    cp_rel = [em("g", lambda g, ch=ch: g.dma_start(
        out=chsl(XT2, ch), in_=spill_h[128 * ch:128 * ch + 128, :]),
        waits=[cp_qk_last], tag=f"rel{ch}", dgroup=f"rl{ch // 2}")
        for ch in range(NCH)]
    cp_r1 = []
    stageF = {0: [cp_wv_cast[-1]], 1: [cp_wv_cast[-1]]}
    slotF = {0: [], 1: []}
    fev_free = [None] * NNT
    for dm in range(NCH):
        par = dm % 2
        slot = (WQS0, WQS1)[par]
        cp_d = em("y", lambda y, dm=dm, par=par: y.dma_start(
            out=WST[:, par * 1024:par * 1024 + 1024].rearrange(
                "p (c j) -> p c j", j=128),
            in_=wo_h[dm].rearrange("c p j -> p c j")),
            waits=stageF[par] + slotF[par] + (cp_q + cp_k if dm < 2 else []),
            tag=f"wod{dm}", dgroup=f"wo{dm // 2}")
        cst = em("v", lambda v, slot=slot, par=par: v.tensor_copy(
            out=slot[:], in_=WST[:, par * 1024:par * 1024 + 1024]),
            waits=[cp_d], tag=f"woc{dm}")
        stageF[par] = [cst]
        reads = []
        for nt in range(NNT):
            last = None
            for ic in range(NCH):
                w = [cst]
                if ic == 0:
                    w.append(fev_free[nt])
                    if dm == 0:
                        w.append(cp_aot[-1])
                last = mm(PS[nt][:, :], slot[:, 128 * ic:128 * ic + 128],
                          chsl(AOT, ic, nt), start=(ic == 0),
                          stop=(ic == NCH - 1), waits=w, tag=f"wo{dm}_{ic}_{nt}")
                reads.append(last)
            ev = em("v", lambda v, dm=dm, nt=nt: v.tensor_tensor(
                out=chsl(XT2, dm, nt), in0=PS[nt][:, :], in1=chsl(XT2, dm, nt),
                op=ALU.add), waits=[last, cp_rel[dm]], tag=f"fev{dm}_{nt}")
            fev_free[nt] = ev
            cp_r1.append(ev)
        slotF[par] = reads

    if "R1" in debug:
        em("g", lambda g: g.dma_start(out=dbg_h["R1"][:], in_=XT2[:]),
           waits=cp_r1)
    if upto == "F":
        return nc, p, dbg_h

    def layernorm_inplace(prev_last, murow, rstdrow, tag):
        sq2 = [[None] * NNT for _ in range(NCH)]
        for ch in range(NCH):
            for nt in range(NNT):
                w = [prev_last] if ch == 0 else [sq2[ch - 1][nt]]
                sq = em("s", lambda s, ch=ch, nt=nt: s.square(
                    out=PBUF[nt][:], in_=chsl(XT2, ch, nt)), waits=w,
                    tag=f"{tag}cb{ch}{nt}")
                sx = mm(PS[nt][0:1, :], ONESCF[:, 0:1], chsl(XT2, ch, nt),
                        start=(ch == 0), stop=(ch == NCH - 1),
                        waits=[sq if ch else prev_last, cp_onescf],
                        tag=f"{tag}sx{ch}{nt}")
                sq2[ch][nt] = mm(PS[4 + nt][0:1, :], ONESCB[:, 0:1], PBUF[nt][:],
                                 start=(ch == 0), stop=(ch == NCH - 1),
                                 waits=[sq], tag=f"{tag}sq{ch}{nt}")
        cp_mu = [None] * NNT
        cp_rs = [None] * NNT
        c3prev = None
        for nt in range(NNT):
            wstat = [sq2[NCH - 1][nt]]
            cp_mu[nt] = em("s", lambda s, nt=nt, murow=murow: s.mul(
                out=murow[0:1, 512 * nt:512 * nt + 512], in_=PS[nt][0:1, :],
                mul=f32(-1.0 / C)), waits=wstat, tag=f"{tag}mu{nt}")
            a1 = em("v", lambda v, nt=nt, murow=murow: v.tensor_tensor(
                out=T0A[0:1, :], in0=murow[0:1, 512 * nt:512 * nt + 512],
                in1=murow[0:1, 512 * nt:512 * nt + 512], op=ALU.mult),
                waits=[cp_mu[nt]], tag=f"{tag}a1{nt}")
            a2 = em("s", lambda s, nt=nt: s.mul(
                out=T0B[0:1, :], in_=PS[4 + nt][0:1, :], mul=f32(1.0 / C)),
                waits=wstat + [c3prev], tag=f"{tag}a2{nt}")
            a3 = em("v", lambda v: v.tensor_tensor(
                out=T0B[0:1, :], in0=T0B[0:1, :], in1=T0A[0:1, :],
                op=ALU.subtract), waits=[a1, a2], tag=f"{tag}a3{nt}")
            a4 = em("s", lambda s: s.activation(
                out=T0B[0:1, :], in_=T0B[0:1, :], func=AF.Sqrt,
                bias=f32(EPS)), waits=[a3], tag=f"{tag}a4{nt}")
            cp_rs[nt] = em("v", lambda v, nt=nt, rstdrow=rstdrow: v.reciprocal(
                out=rstdrow[0:1, 512 * nt:512 * nt + 512], in_=T0B[0:1, :]),
                waits=[a4], tag=f"{tag}rs{nt}")
            c3prev = cp_rs[nt]
        ap_last = []
        ap_nt = [None] * NNT
        ap_ch = [None] * NCH
        bank_free = {0: cp_mu[0], 1: cp_mu[1], 2: cp_mu[2], 3: cp_mu[3]}
        for nt in range(NNT):
            b = 2 * (nt % 2)
            mub = mm(PS[b][:, :], ONESR[0:1, 0:128],
                     murow[0:1, 512 * nt:512 * nt + 512], start=True, stop=True,
                     waits=[cp_mu[nt], bank_free[b]], tag=f"{tag}mub{nt}")
            rsb = mm(PS[b + 1][:, :], ONESR[0:1, 0:128],
                     rstdrow[0:1, 512 * nt:512 * nt + 512], start=True,
                     stop=True, waits=[cp_rs[nt], bank_free[b + 1]],
                     tag=f"{tag}rsb{nt}")
            s1 = s2 = None
            for ch in range(NCH):
                s1 = em("v", lambda v, ch=ch, nt=nt, b=b: v.tensor_tensor(
                    out=chsl(XT2, ch, nt), in0=chsl(XT2, ch, nt),
                    in1=PS[b][:, :], op=ALU.add), waits=[mub],
                    tag=f"{tag}p1_{ch}_{nt}")
                s2 = em("v", lambda v, ch=ch, nt=nt, b=b: v.tensor_tensor(
                    out=chsl(XT2, ch, nt), in0=chsl(XT2, ch, nt),
                    in1=PS[b + 1][:, :], op=ALU.mult), waits=[rsb, s1],
                    tag=f"{tag}p2_{ch}_{nt}")
                ap_last.append(s2)
                ap_ch[ch] = s2
            ap_nt[nt] = s2
            bank_free[b] = s1
            bank_free[b + 1] = s2
        return ap_last, ap_nt, ap_ch

    cp_y, cp_y_nt, _ = layernorm_inplace(cp_r1[-1], MU2, RSTD2, "ga")
    if "Y" in debug:
        em("g", lambda g: g.dma_start(out=dbg_h["Y"][:], in_=XT2[:]), waits=cp_y)
    if upto == "G":
        return nc, p, dbg_h

    def mix(dst, mucol, mu1col, cp_mcol, cp_mcol2, prev_nt, tag):
        cps = []
        cps_nt = [None] * NNT
        for nt in range(NNT):
            for ch in range(NCH):
                base = 2048 * ch + 512 * nt
                lo = 1 if nt == 0 else 0
                t1 = em("v", lambda v, base=base, ch=ch, dst=dst,
                        mucol=mucol: v.tensor_scalar_mul(
                    out=dst[:, base:base + 512], in0=XT2[:, base:base + 512],
                    scalar1=mucol[:, ch:ch + 1]),
                    waits=[prev_nt[nt], cp_mcol], tag=f"{tag}a{ch}_{nt}")
                t2 = em("v", lambda v, base=base, lo=lo, ch=ch, dst=dst,
                        mu1col=mu1col: v.scalar_tensor_tensor(
                    out=dst[:, base + lo:base + 512],
                    in0=XT2[:, base + lo - 1:base + 511],
                    scalar=mu1col[:, ch:ch + 1],
                    in1=dst[:, base + lo:base + 512],
                    op0=ALU.mult, op1=ALU.add), waits=[t1, cp_mcol2],
                    tag=f"{tag}m{ch}_{nt}")
                cps.append(t2)
            cps_nt[nt] = cps[-1]
        return cps, cps_nt

    cp_xk, cp_xk_nt = mix(XK, MUKC, MUKC2, cp_muk, cp_muk2, cp_y_nt, "mk")
    if "XK" in debug:
        em("g", lambda g: g.dma_start(out=dbg_h["XK"][:], in_=XK[:]),
           waits=cp_xk)

    cp_kact = []
    cp_kact_ntd = [None] * NNT
    stageH = {0: [stageF[0][-1]], 1: [stageF[1][-1]]}
    slotH = {0: list(slotF[0]), 1: list(slotF[1])}
    t0_free = {0: None, 1: None}
    bank_freeH = [None] * NNT
    for dm in range(NCH):
        par = dm % 2
        slot = (KWB0, KWB1)[par]
        cp_d = em("y", lambda y, dm=dm, par=par: y.dma_start(
            out=WST[:, par * 1024:par * 1024 + 1024].rearrange(
                "p (c j) -> p c j", j=128),
            in_=wkf_h[dm].rearrange("c p j -> p c j")),
            waits=stageH[par] + slotH[par],
            tag=f"kwd{dm}", dgroup=f"kw{dm // 2}")
        cst = em("v", lambda v, slot=slot, par=par: v.tensor_copy(
            out=slot[:], in_=WST[:, par * 1024:par * 1024 + 1024]),
            waits=[cp_d], tag=f"kwc{dm}")
        stageH[par] = [cst]
        reads = []
        for nt in range(NNT):
            last = None
            for cc in range(NCH):
                w = [cst]
                if cc == 0:
                    w.append(bank_freeH[nt])
                    w.append(cp_xk_nt[nt])
                last = mm(PS[nt][:, :], slot[:, 128 * cc:128 * cc + 128],
                          chsl(XK, cc, nt), start=(cc == 0),
                          stop=(cc == NCH - 1), waits=w, tag=f"kp{dm}_{cc}_{nt}")
                reads.append(last)
            tp = nt % 2
            t0 = (T0A, T0B)[tp]
            r1 = em("s", lambda s, t0=t0, nt=nt: s.activation(
                out=t0[:], in_=PS[nt][:, :], func=AF.Relu),
                waits=[last, t0_free[tp]], tag=f"kr{dm}_{nt}")
            bank_freeH[nt] = r1
            ks = em("p", lambda g, dm=dm, nt=nt, t0=t0: g.tensor_tensor(
                out=chsl(KACT, dm, nt), in0=t0[:], in1=t0[:], op=ALU.mult),
                waits=[r1], tag=f"ks{dm}_{nt}")
            t0_free[tp] = ks
            cp_kact.append(ks)
            cp_kact_ntd[nt] = ks
        slotH[par] = reads
    if "KACT" in debug:
        em("g", lambda g: g.dma_start(out=dbg_h["KACT"][:], in_=KACT[:]),
           waits=cp_kact)

    cp_xr, cp_xr_nt = mix(XR, MURC, MURC2, cp_mur, cp_mur2, cp_kact_ntd, "mr")
    if "XR" in debug:
        em("g", lambda g: g.dma_start(out=dbg_h["XR"][:], in_=XR[:]),
           waits=cp_xr)

    cp_r2 = []
    wst_r = [stageH[0][-1], stageH[1][-1]]
    wst_v = [stageH[0][-1], stageH[1][-1]]
    rwb_readers = []
    vwb_readers = []
    t0a_free = None
    t0b_free = None
    bank4 = {0: None, 1: None, 2: None, 3: None}
    for dm in range(NCH):
        cp_dr = em("y", lambda y, dm=dm: y.dma_start(
            out=WST[:, 0:1024].rearrange("p (c j) -> p c j", j=128),
            in_=wr_h[dm].rearrange("c p j -> p c j")), waits=wst_r + rwb_readers, tag=f"rwd{dm}", dgroup=f"h4w{dm}")
        cr = em("s", lambda s: s.copy(out=RWB[:], in_=WST[:, 0:1024]),
                waits=[cp_dr], tag=f"rwc{dm}")
        wst_r = [cr]
        cp_dv = em("y", lambda y, dm=dm: y.dma_start(
            out=WST[:, 1024:2048].rearrange("p (c j) -> p c j", j=128),
            in_=wvf_h[dm].rearrange("c p j -> p c j")), waits=wst_v + vwb_readers, tag=f"vwd{dm}", dgroup=f"h4w{dm}")
        cv = em("v", lambda v: v.tensor_copy(out=VWB[:], in_=WST[:, 1024:2048]),
                waits=[cp_dv], tag=f"vwc{dm}")
        wst_v = [cv]
        rwb_readers = []
        vwb_readers = []
        for nt in range(NNT):
            b = nt % 2
            lastr = lastv = None
            for cc in range(NCH):
                w1 = [cr]
                w2 = [cv]
                if cc == 0:
                    w1.append(bank4[b])
                    w2.append(bank4[2 + b])
                    w1.append(cp_xr_nt[nt])
                lastr = mm(PS[b][:, :], RWB[:, 128 * cc:128 * cc + 128],
                           chsl(XR, cc, nt), start=(cc == 0),
                           stop=(cc == NCH - 1), waits=w1,
                           tag=f"rp{dm}_{cc}_{nt}")
                lastv = mm(PS[2 + b][:, :], VWB[:, 128 * cc:128 * cc + 128],
                           chsl(KACT, cc, nt), start=(cc == 0),
                           stop=(cc == NCH - 1), waits=w2,
                           tag=f"v2{dm}_{cc}_{nt}")
                rwb_readers.append(lastr)
                vwb_readers.append(lastv)
            sg = em("s", lambda s, b=b: s.activation(
                out=T0A[:], in_=PS[b][:, :], func=AF.Sigmoid),
                waits=[lastr, t0a_free], tag=f"sg{dm}_{nt}")
            bank4[b] = sg
            g1 = em("v", lambda v, b=b: v.tensor_tensor(
                out=T0B[:], in0=PS[2 + b][:, :], in1=T0A[:], op=ALU.mult),
                waits=[sg, lastv, t0b_free], tag=f"g1{dm}_{nt}")
            t0a_free = g1
            bank4[2 + b] = g1
            rr = em("v", lambda v, dm=dm, nt=nt: v.tensor_tensor(
                out=chsl(XT2, dm, nt), in0=T0B[:], in1=chsl(XT2, dm, nt),
                op=ALU.add), waits=[g1], tag=f"r2_{dm}_{nt}")
            t0b_free = rr
            cp_r2.append(rr)
    if upto == "H":
        pass

    cp_out, _, cp_out_ch = layernorm_inplace(cp_r2[-1], MU2, RSTD2, "gb")
    for ch in range(NCH):
        em("g", lambda g, ch=ch: g.dma_start(
            out=out_h[128 * ch:128 * ch + 128, :], in_=chsl(XT2, ch)),
            waits=[cp_out_ch[ch]], tag=f"odma{ch}")
    return nc, p, dbg_h


def finalize(nc, p):
    from contextlib import ExitStack
    refs = set()
    for e in p.ops:
        for fn, ws, key, c, tag, dg in p.ops[e]:
            for we, wv in ws:
                refs.add((we, wv))
    keymap = {}
    misc_total = 0
    for q in ("y", "g"):
        open_group = {}
        for op in p.ops[q]:
            if len(op) < 6:
                continue
            key, dg = op[2], op[5]
            if key in ("p",):
                continue
            if (key, 16) not in refs:
                keymap[key] = "dmisc"
                misc_total += 16
                op[3] = misc_total
            elif dg is not None and dg in open_group:
                lead = open_group.pop(dg)
                keymap[key] = lead
                op[3] = 32
            else:
                keymap[key] = key
                if dg is not None:
                    open_group[dg] = key
    eng_refs = {k: set() for k in ("t", "v", "s", "p")}
    for we, wv in refs:
        if we in eng_refs:
            eng_refs[we].add(wv)
    remap = {k: {v: i + 1 for i, v in enumerate(sorted(eng_refs[k]))}
             for k in eng_refs}
    dma_keys = sorted({keymap[op[2]] for q in ("y", "g") for op in p.ops[q]
                       if op[2] != "p"})
    finals = {}
    for q in ("y", "g"):
        for op in p.ops[q]:
            if op[2] == "p":
                continue
            k = keymap[op[2]]
            finals[k] = max(finals.get(k, 0), op[3])
    with ExitStack() as st:
        sems = {k: st.enter_context(nc.semaphore(f"sem_{k}"))
                for k in ["t", "v", "s", "p"] + dma_keys}
        block = st.enter_context(nc.Block())

        def rwait(eng, hw, we, wv):
            k = keymap.get(we, we)
            if we in remap:
                rv = remap[we][wv]
            else:
                rv = finals[k]
            if hw.get(k, 0) < rv:
                eng.wait_ge(sems[k], rv)
                hw[k] = rv

        def make_runner(e):
            ops = p.ops[e]

            def run(eng):
                hw = {}
                for fn, ws, key, c, tag, dg in ops:
                    for we, wv in ws:
                        rwait(eng, hw, we, wv)
                    ins = fn(eng)
                    if key in ("t", "v", "s", "p"):
                        if c in remap[key] and ins is not None:
                            ins.then_inc(sems[key], 1)
                    else:
                        ins.then_inc(sems[keymap[key]], 16)
                if e == "g":
                    for k, v in sorted(finals.items()):
                        if hw.get(k, 0) < v:
                            eng.wait_ge(sems[k], v)
            return run

        block.tensor(make_runner("t"))
        block.vector(make_runner("v"))
        block.scalar(make_runner("s"))
        block.sync(make_runner("y"))
        block.gpsimd(make_runner("g"))
    return nc


def host_inputs(full):
    x = np.ascontiguousarray(full["L0_attn_x"], dtype=np.float32)
    Wq = np.ascontiguousarray(full["L0_attn_Wq"], dtype=np.float32)
    Wkv = np.ascontiguousarray(full["L0_attn_Wkv"], dtype=np.float32)
    Wb = np.ascontiguousarray(full["L0_attn_W_bilinear"], dtype=np.float32)
    Wo = np.ascontiguousarray(full["L0_attn_Wo"], dtype=np.float32)
    Wr = np.ascontiguousarray(full["L0_ffn_W_r"], dtype=np.float32)
    Wkf = np.ascontiguousarray(full["L0_ffn_W_k"], dtype=np.float32)
    Wvf = np.ascontiguousarray(full["L0_ffn_W_v"], dtype=np.float32)
    gam = np.ascontiguousarray(full["L0_attn_gamma"], dtype=np.float32)
    mur = np.ascontiguousarray(full["L0_ffn_mu_r"], dtype=np.float32)
    muk = np.ascontiguousarray(full["L0_ffn_mu_k"], dtype=np.float32)

    def dm_tiles(W):
        return np.ascontiguousarray(
            W.reshape(8, 128, 8, 128).transpose(2, 0, 1, 3))

    cos2, sin2 = _rope_tables()
    shared = {
        "wq": dm_tiles(Wq),
        "wk": dm_tiles(Wkv[:, 0:C]),
        "wv": np.ascontiguousarray(Wkv[:, C:2 * C].reshape(8, 128, 1024)),
        "wo": dm_tiles(Wo),
        "wb": np.ascontiguousarray(Wb.reshape(8, 128, 64)),
        "wr": dm_tiles(np.ascontiguousarray(Wr.T)),
        "wkf": dm_tiles(np.ascontiguousarray(Wkf.T)),
        "wvf": dm_tiles(np.ascontiguousarray(Wvf.T)),
        "gam": np.ascontiguousarray(gam.reshape(8, 128).T),
        "mur": np.ascontiguousarray(mur.reshape(8, 128).T),
        "muk": np.ascontiguousarray(muk.reshape(8, 128).T),
        "cos2": cos2,
        "sin2": sin2,
        "r2t": _perm_r2t(),
        "epat": _epat(),
        "id8": np.eye(8, dtype=np.float32),
        "onesr": np.ones((1, 128), dtype=np.float32),
    }
    in_maps = []
    for c in range(8):
        m = dict(shared)
        m["xt"] = np.ascontiguousarray(x[c].T)
        in_maps.append(m)
    return in_maps


_NC_CACHE = {}


def kernel(**inputs):
    if "main" not in _NC_CACHE:
        nc, p, dbg = build_nc()
        _NC_CACHE["main"] = finalize(nc, p)
    nc = _NC_CACHE["main"]
    in_maps = host_inputs(inputs)
    res = run_bass_kernel_spmd(nc, in_maps, core_ids=list(range(8)))
    outs = [np.ascontiguousarray(r["out"].T) for r in res.results]
    return np.stack(outs, axis=0)


# version 16 (final)
# speedup vs baseline: 1.0259x; 1.0008x over previous
import numpy as np
import ml_dtypes
import concourse.bass as bass
import concourse.mybir as mybir
from concourse.bass_utils import run_bass_kernel_spmd

F32 = mybir.dt.float32
BF16 = mybir.dt.bfloat16
AF = mybir.ActivationFunctionType
ALU = mybir.AluOpType
AX = mybir.AxisListType.X
f32 = np.float32

T = 2048
C = 1024
DH = 64
NCH = 8
NNT = 4
NTC = 16
NPR = 8
EPS = 1e-5

class Prog:

    def __init__(self):
        self.ops = {e: [] for e in ("t", "v", "s", "y", "g")}
        self.cnt = {e: 0 for e in ("t", "v", "s", "p")}
        self.nd = 0

    def em(self, e, fn, waits=(), tag="", dgroup=None):
        ws = []
        for w in waits:
            if w is None:
                continue
            we, wv = w
            if we == e and e != "p" and wv < self.cnt[e] - 1:
                continue
            if we in ("t", "v", "s"):
                assert wv <= self.cnt[we], f"future {we}:{wv} ({tag})"
            ws.append((we, wv))
        if e in ("y", "g"):
            key = f"d{self.nd}"
            self.nd += 1
            self.ops[e].append([fn, ws, key, 16, tag, dgroup])
            tok = (key, 16)
        elif e == "p":
            self.ops["g"].append([fn, ws, "p", self.cnt["p"] + 1, tag, None])
            self.cnt["p"] += 1
            tok = ("p", self.cnt["p"])
        else:
            self.ops[e].append([fn, ws, e, self.cnt[e] + 1, tag, None])
            self.cnt[e] += 1
            tok = (e, self.cnt[e])
        return tok


def _rope_tables():
    theta = 1.0 / (10000.0 ** (np.arange(0, DH, 2, dtype=np.float64) / DH))
    pos = np.arange(T, dtype=np.float64)
    freqs = pos[:, None] * theta[None, :]
    emb = np.concatenate([freqs, freqs], axis=-1)
    cosT = np.cos(emb).T.astype(np.float32)
    sinT = np.sin(emb).T.astype(np.float32)
    return (np.ascontiguousarray(np.concatenate([cosT, cosT], axis=0)),
            np.ascontiguousarray(np.concatenate([sinT, sinT], axis=0)))


def _perm_r2t():
    R = np.zeros((DH, DH), dtype=np.float32)
    for m in range(DH // 2):
        R[2 * m, 2 * m + 1] = -1.0
        R[2 * m + 1, 2 * m] = 1.0
    R2 = np.zeros((128, 128), dtype=np.float32)
    R2[0:64, 0:64] = R
    R2[64:128, 64:128] = R
    return np.ascontiguousarray(R2.T)


def _epat():
    E = np.zeros((128, 8 * NPR), dtype=np.float32)
    for ch in range(NPR):
        E[:, 8 * ch + ch] = 1.0
    return E


def build_nc(debug=(), upto="ALL", canary=True):
    nc = bass.Bass("TRN2", target_bir_lowering=False)
    p = Prog()
    em = p.em

    xt_h = nc.dram_tensor("xt", [C, T], F32, kind="ExternalInput")
    out_h = nc.dram_tensor("out", [C, T], F32, kind="ExternalOutput")
    wq_h = nc.dram_tensor("wq", [8, 8, 128, 128], F32, kind="ExternalInput")
    wk_h = nc.dram_tensor("wk", [8, 8, 128, 128], F32, kind="ExternalInput")
    wv_h = nc.dram_tensor("wv", [8, 128, 1024], F32, kind="ExternalInput")
    wo_h = nc.dram_tensor("wo", [8, 8, 128, 128], F32, kind="ExternalInput")
    wb_h = nc.dram_tensor("wb", [8, 128, 64], F32, kind="ExternalInput")
    wr_h = nc.dram_tensor("wr", [8, 8, 128, 128], F32, kind="ExternalInput")
    wkf_h = nc.dram_tensor("wkf", [8, 8, 128, 128], F32, kind="ExternalInput")
    wvf_h = nc.dram_tensor("wvf", [8, 8, 128, 128], F32, kind="ExternalInput")
    gam_h = nc.dram_tensor("gam", [128, 8], F32, kind="ExternalInput")
    mur_h = nc.dram_tensor("mur", [128, 8], F32, kind="ExternalInput")
    muk_h = nc.dram_tensor("muk", [128, 8], F32, kind="ExternalInput")
    cos_h = nc.dram_tensor("cos2", [128, T], F32, kind="ExternalInput")
    sin_h = nc.dram_tensor("sin2", [128, T], F32, kind="ExternalInput")
    r2t_h = nc.dram_tensor("r2t", [128, 128], F32, kind="ExternalInput")
    epat_h = nc.dram_tensor("epat", [128, 64], F32, kind="ExternalInput")
    id8_h = nc.dram_tensor("id8", [8, 8], F32, kind="ExternalInput")
    onesr_h = nc.dram_tensor("onesr", [1, 128], F32, kind="ExternalInput")
    spill_h = nc.dram_tensor("xspill", [C, T], F32)

    dbg_h = {}
    for name, shape, dt in (
        ("XBF", [128, 16384], BF16), ("QT", [128, 16384], BF16),
        ("KTT", [128, 16384], BF16), ("VT", [128, 16640], BF16),
        ("AOT", [128, 16384], BF16), ("R1", [128, 16384], F32),
        ("Y", [128, 16384], F32), ("XK", [128, 16384], BF16),
        ("KACT", [128, 16384], BF16), ("XR", [128, 16384], BF16),
        ("NMU", [1, 2048], F32), ("RSTD", [1, 2048], F32),
        ("CVEC", [128, 8], F32), ("QPRE6", [128, 2048], BF16),
        ("WBB", [128, 512], BF16),
        ("WQS6", [128, 1024], BF16), ("WST6", [128, 2048], F32),
    ):
        if name in debug:
            dbg_h[name] = nc.dram_tensor("dbg_" + name, shape, dt,
                                         kind="ExternalOutput")

    A0 = nc.alloc_sbuf_tensor_at
    SB0 = 16512
    def A(name, shape, dt, offset):
        return A0(name, shape, dt, offset=SB0 + offset)
    XT = A("XT", [128, 16384], F32, offset=0)
    QT = A("QT", [128, 16384], BF16, offset=0)
    KTT = A("KTT", [128, 16384], BF16, offset=32768)
    XT2 = A("XT2", [128, 16384], F32, offset=0)
    XBF = A("XBF", [128, 16384], BF16, offset=65536)
    AOT = A("AOT", [128, 16384], BF16, offset=65536)
    XR = A("XR", [128, 16384], BF16, offset=65536)
    VT = A("VT", [128, 16640], BF16, offset=98304)
    XK = A("XK", [128, 16384], BF16, offset=98304)
    WVB = A("WVB", [128, 8192], BF16, offset=131584)
    KACT = A("KACT", [128, 16384], BF16, offset=131584)
    BOUNCE = A("BOUNCE", [1, 1024], F32, offset=131584)
    CPROW = A("CPROW", [1, 16], F32, offset=135616)
    NMU = A("NMU", [1, 2048], F32, offset=147968)
    NCKV = A("NCKV", [1, 2048], F32, offset=156160)
    WST = A("WST", [128, 2048], F32, offset=164352)
    COS = A("COS", [128, 2048], F32, offset=172544)
    SIN = A("SIN", [128, 2048], F32, offset=180736)
    MU2 = A("MU2", [1, 2048], F32, offset=172544)
    RSTD2 = A("RSTD2", [1, 2048], F32, offset=180736)
    WQS0 = A("WQS0", [128, 1024], BF16, offset=188928)
    WQS1 = A("WQS1", [128, 1024], BF16, offset=190976)
    WKS0 = A("WKS0", [128, 1024], BF16, offset=193024)
    WKS1 = A("WKS1", [128, 1024], BF16, offset=195072)
    KWB0, KWB1 = WQS0, WQS1
    RWB, VWB = WKS0, WKS1
    QPRE = A("QPRE", [128, 2048], BF16, offset=197120)
    QPRE2 = A("QPRE2", [128, 2048], BF16, offset=98304)
    MIXT = A("MIXT", [128, 512], F32, offset=197120)
    T0A = A("T0A", [128, 512], F32, offset=201216)
    T0B = A("T0B", [128, 512], F32, offset=203264)
    T0W = A("T0W", [128, 1024], F32, offset=201216)
    PBIG = [A(f"PBIG{i}", [128, 1024], BF16, offset=205312 + 2048 * i)
            for i in range(2)]
    PBIG += [A(f"PBIG{i + 2}", [128, 1024], BF16, offset=135680 + 2048 * i)
             for i in range(2)]
    PBUF = [PBIG[i // 2][:, 512 * (i % 2):512 * (i % 2) + 512]
            for i in range(8)]
    GAMC = A("GAMC", [128, 8], F32, offset=209408)
    MURC = A("MURC", [128, 8], F32, offset=209440)
    MUKC = A("MUKC", [128, 8], F32, offset=209472)
    RSTDC = A("RSTDC", [128, 16], F32, offset=209504)
    CVEC = A("CVEC", [128, 8], F32, offset=209568)
    CP8 = A("CP8", [8, 2], F32, offset=209600)
    Q2M = A("Q2M", [8, 8], F32, offset=209632)
    Z2M = A("Z2M", [8, 8], F32, offset=209664)
    ONESR = A("ONESR", [1, 128], F32, offset=209696)
    ONESCB = A("ONESCB", [128, 1], BF16, offset=210208)
    ONESCF = A("ONESCF", [128, 1], F32, offset=211712)
    ID8 = A("ID8", [8, 8], F32, offset=210240)
    R2TB = A("R2TB", [128, 128], BF16, offset=210272)
    EPATB = A("EPATB", [128, 64], BF16, offset=210528)
    WBB = A("WBB", [128, 512], BF16, offset=210656)
    MUKC2 = A("MUKC2", [128, 8], F32, offset=211744)
    MURC2 = A("MURC2", [128, 8], F32, offset=211776)

    PSA = nc.alloc_psum_tensor("psA", [128, 1024], F32)
    PSB = nc.alloc_psum_tensor("psB", [128, 1024], F32)
    PSC = nc.alloc_psum_tensor("psC", [128, 1024], F32)
    PSD = nc.alloc_psum_tensor("psD", [128, 1024], F32)
    PS = [PSA[:, 0:512], PSA[:, 512:1024], PSB[:, 0:512], PSB[:, 512:1024],
          PSC[:, 0:512], PSC[:, 512:1024], PSD[:, 0:512], PSD[:, 512:1024]]
    PSAB = [PSA, PSB]

    def chsl(buf, ch, nt=None):
        if nt is None:
            return buf[:, 2048 * ch: 2048 * ch + 2048]
        return buf[:, 2048 * ch + 512 * nt: 2048 * ch + 512 * nt + 512]

    def mm(psap, lhsT, rhs, start, stop, waits=(), tp=None, tag=""):
        return em("t", lambda t, a=psap, l=lhsT, r=rhs, s0=start, s1=stop, tpp=tp:
                  t.matmul(a, l, r, start=s0, stop=s1, tile_position=tpp),
                  waits=waits, tag=tag)

    cp_cos = em("g", lambda g: g.dma_start(out=COS[:], in_=cos_h[:]), dgroup="tbl")
    cp_sin = em("g", lambda g: g.dma_start(out=SIN[:], in_=sin_h[:]), dgroup="tbl")
    cp_gam = em("y", lambda y: y.dma_start(out=GAMC[:], in_=gam_h[:]), dgroup="c0a")
    cp_mur = em("y", lambda y: y.dma_start(out=MURC[:], in_=mur_h[:]), dgroup="c0a")
    cp_muk = em("y", lambda y: y.dma_start(out=MUKC[:], in_=muk_h[:]), dgroup="c0b")
    cp_muk2 = em("s", lambda s: s.activation(
        out=MUKC2[:], in_=MUKC[:], func=AF.Copy, scale=f32(-1.0),
        bias=1.0), waits=[cp_muk], tag="muk2")
    cp_mur2 = em("s", lambda s: s.activation(
        out=MURC2[:], in_=MURC[:], func=AF.Copy, scale=f32(-1.0),
        bias=1.0), waits=[cp_mur], tag="mur2")
    cp_id8 = em("y", lambda y: y.dma_start(out=ID8[:], in_=id8_h[:]), dgroup="c0b")
    cp_onr = em("y", lambda y: y.dma_start(out=ONESR[:], in_=onesr_h[:]))
    cp_r2ts = em("y", lambda y: y.dma_start(out=WST[:, 0:128], in_=r2t_h[:]), dgroup="c0c")
    cp_epts = em("y", lambda y: y.dma_start(out=WST[:, 128:192], in_=epat_h[:]), dgroup="c0c")
    cp_r2tb = em("v", lambda v: v.tensor_copy(out=R2TB[:], in_=WST[:, 0:128]),
                 waits=[cp_r2ts])
    cp_eptb = em("v", lambda v: v.tensor_copy(out=EPATB[:], in_=WST[:, 128:192]),
                 waits=[cp_epts])
    cp_onescb = em("v", lambda v: v.memset(ONESCB[:], 1.0))
    cp_onescf = em("v", lambda v: v.memset(ONESCF[:], 1.0))
    cp_can = None
    if canary:
        em("v", lambda v: v.memset(QPRE[:], 0.0))
        em("v", lambda v: v.memset(QPRE2[:], 0.0))
        for _pb in PBIG:
            em("v", lambda v, _pb=_pb: v.memset(_pb[:], 0.0))
        em("v", lambda v: v.memset(T0A[:], 0.0))
        cp_can = em("v", lambda v: v.memset(T0B[:], 0.0))
    cp_wbs = []
    for pr in range(NPR):
        cpw = em("y", lambda y, pr=pr: y.dma_start(
            out=WST[:, 192 + 64 * pr: 256 + 64 * pr], in_=wb_h[pr]),
            dgroup=f"wb{pr // 2}")
        cp_wbs.append(em("v", lambda v, pr=pr: v.tensor_copy(
            out=WBB[:, 64 * pr: 64 * pr + 64],
            in_=WST[:, 192 + 64 * pr: 256 + 64 * pr]), waits=[cpw]))
    cp_wst_free = [cp_r2tb, cp_eptb] + cp_wbs

    cp_x = [em("g", lambda g, ch=ch: g.dma_start(
        out=chsl(XT, ch), in_=xt_h[128 * ch:128 * ch + 128, :]),
        tag=f"xin{ch}", dgroup=f"xin{ch // 2}")
        for ch in range(NCH)]

    cp_sq2 = [[None] * NNT for _ in range(NCH)]
    cp_sx = [[None] * NNT for _ in range(NCH)]
    for ch in range(NCH):
        for nt in range(NNT):
            w = [cp_x[ch]]
            if ch > 0:
                w.append(cp_sq2[ch - 1][nt])
            sq = em("s", lambda s, ch=ch, nt=nt: s.square(
                out=PBUF[nt][:], in_=chsl(XT, ch, nt)),
                waits=w + ([cp_can] if ch == 0 else []),
                tag=f"asq_{ch}{nt}")
            cp_sx[ch][nt] = mm(PS[nt][0:1, :], ONESCF[:, 0:1],
                               chsl(XT, ch, nt),
                               start=(ch == 0), stop=(ch == NCH - 1),
                               waits=[cp_x[ch], cp_onescf], tag=f"asx{ch}{nt}")
            cp_sq2[ch][nt] = mm(PS[4 + nt][0:1, :], ONESCB[:, 0:1], PBUF[nt][:],
                                start=(ch == 0), stop=(ch == NCH - 1),
                                waits=[sq, cp_onescb], tag=f"asq{ch}{nt}")

    cp_nmu = [None] * NNT
    cp_rstd = [None] * NNT
    cp_c3 = [None] * NNT
    for nt in range(NNT):
        wstat = [cp_sq2[NCH - 1][nt]]
        cp_nmu[nt] = em("s", lambda s, nt=nt: s.mul(
            out=NMU[0:1, 512 * nt:512 * nt + 512], in_=PS[nt][0:1, :],
            mul=f32(-1.0 / C)), waits=wstat, tag=f"nmu{nt}")
        c1 = em("v", lambda v, nt=nt: v.tensor_tensor(
            out=T0A[0:1, :], in0=NMU[0:1, 512 * nt:512 * nt + 512],
            in1=NMU[0:1, 512 * nt:512 * nt + 512], op=ALU.mult),
            waits=[cp_nmu[nt]], tag=f"bmu2{nt}")
        c2 = em("s", lambda s, nt=nt: s.mul(
            out=T0B[0:1, :], in_=PS[4 + nt][0:1, :], mul=f32(1.0 / C)),
            waits=wstat + [cp_rstd[nt - 1] if nt else cp_can], tag=f"bmsq{nt}")
        cp_c3[nt] = em("v", lambda v: v.tensor_tensor(
            out=T0B[0:1, :], in0=T0B[0:1, :], in1=T0A[0:1, :],
            op=ALU.subtract), waits=[c1, c2], tag=f"bvar{nt}")
        c4 = em("s", lambda s: s.activation(
            out=T0B[0:1, :], in_=T0B[0:1, :], func=AF.Sqrt, bias=f32(EPS)),
            waits=[cp_c3[nt]], tag=f"bsd{nt}")
        cp_rstd[nt] = em("v", lambda v, nt=nt: v.reciprocal(
            out=NCKV[0:1, 512 * nt:512 * nt + 512], in_=T0B[0:1, :]),
            waits=[c4], tag=f"brstd{nt}")
    cp_fold_sin = [None] * NNT
    cp_fold = []
    cp_xbf = []
    cp_spill = [None] * NCH
    cp_mub = [None] * NNT
    for nt in range(NNT):
        rb = mm(PS[nt][:, :], ONESR[0:1, 0:128],
                NCKV[0:1, 512 * nt:512 * nt + 512], start=True, stop=True,
                waits=[cp_rstd[nt], cp_onr, cp_nmu[nt]], tag=f"brb{nt}")
        fc = em("v", lambda v, nt=nt: v.tensor_tensor(
            out=COS[:, 512 * nt:512 * nt + 512],
            in0=COS[:, 512 * nt:512 * nt + 512], in1=PS[nt][:, :],
            op=ALU.mult), waits=[rb, cp_cos], tag=f"bfc{nt}")
        fs = em("v", lambda v, nt=nt: v.tensor_tensor(
            out=SIN[:, 512 * nt:512 * nt + 512],
            in0=SIN[:, 512 * nt:512 * nt + 512], in1=PS[nt][:, :],
            op=ALU.mult), waits=[rb, cp_sin], tag=f"bfs{nt}")
        cp_fold_sin[nt] = fs
        cp_fold += [fc, fs]
        cp_mub[nt] = mm(PS[4 + nt][:, :], ONESR[0:1, 0:128],
                        NMU[0:1, 512 * nt:512 * nt + 512], start=True,
                        stop=True,
                        waits=[cp_nmu[nt], cp_rstd[nt], cp_onr],
                        tag=f"bmub{nt}")
    xbf_by_ch = [[None] * NNT for _ in range(NCH)]
    for ch in range(NCH):
        for nt in range(NNT):
            t = em("v", lambda v, ch=ch, nt=nt: v.tensor_tensor(
                out=chsl(XBF, ch, nt), in0=chsl(XT, ch, nt),
                in1=PS[4 + nt][:, :], op=ALU.add), waits=[cp_mub[nt]],
                tag=f"xbf{ch}_{nt}")
            xbf_by_ch[ch][nt] = t
            cp_xbf.append(t)
        cp_spill[ch] = em("g", lambda g, ch=ch: g.dma_start(
            out=spill_h[128 * ch:128 * ch + 128, :], in_=chsl(XT, ch)),
            waits=xbf_by_ch[ch], tag=f"spill{ch}", dgroup=f"sp{ch // 2}")
    rc_last = None
    for tc in range(NTC):
        rc_last = mm(PS[0][:, tc:tc + 1], NCKV[0:1, 128 * tc:128 * tc + 128],
                     ONESR[0:1, 0:1], start=True, stop=True,
                     waits=cp_rstd + cp_fold, tag=f"brc{tc}")
    cp_rstdc = em("s", lambda s: s.copy(out=RSTDC[:], in_=PS[0][:, 0:16]),
                  waits=[rc_last], tag="brcev")

    if "NMU" in debug:
        em("g", lambda g: g.dma_start(out=dbg_h["NMU"][:], in_=NMU[0:1, :]),
           waits=cp_nmu)
    if "RSTD" in debug:
        em("g", lambda g: g.dma_start(out=dbg_h["RSTD"][:], in_=NCKV[0:1, :]),
           waits=cp_rstd + [cp_rstdc])
    if "XBF" in debug:
        em("g", lambda g: g.dma_start(out=dbg_h["XBF"][:], in_=XBF[:]),
           waits=cp_xbf)
    if upto == "B":
        return nc, p, dbg_h

    state = {
        "bank_free": {nt: (cp_rstdc if nt == 0 else cp_fold_sin[nt])
                      for nt in range(NNT)},
        "rot_free": {6: cp_xbf[-2], 7: cp_xbf[-1]},
        "qpre_readers": [[[] for _ in range(NNT)] for _ in range(2)],
        "bil_free": {0: None, 1: None},
        "t0_reader": None,
    }

    stage_state = {0: [], 1: []}

    def proj_qk(which, out_tokens):
        wtile_h = wq_h if which == "q" else wk_h
        slots = (WQS0, WQS1) if which == "q" else (WKS0, WKS1)
        slot_readers = {0: [], 1: []}
        stage_readers = stage_state
        for oc in range(NCH):
            par = oc % 2
            slot = slots[par]
            half = par * 1024
            stg = WST[:, half:half + 1024].rearrange("p (c j) -> p c j", j=128)
            wd = stage_readers[par] + (cp_wst_free if which == "q" and oc < 2 else [])
            cp_d = em("y", lambda y, stg=stg, oc=oc: y.dma_start(
                out=stg, in_=wtile_h[oc].rearrange("c p j -> p c j")),
                waits=wd, tag=f"{which}wd{oc}", dgroup=f"{which}w{oc // 2}")
            casts = []
            for cc in range(NCH):
                casts.append(em("v", lambda v, slot=slot, cc=cc, half=half:
                               v.tensor_scalar_mul(
                    out=slot[:, 128 * cc:128 * cc + 128],
                    in0=WST[:, half + 128 * cc: half + 128 * cc + 128],
                    scalar1=GAMC[:, cc:cc + 1]),
                    waits=[cp_d, cp_gam] + slot_readers[par],
                    tag=f"{which}wc{oc}_{cc}"))
            stage_readers[par] = [casts[-1]]
            reads = []
            last = [None] * NNT
            for cc in range(NCH):
                for nt in range(NNT):
                    w = [casts[cc]]
                    if cc == 0:
                        w.append(state["bank_free"][nt])
                    tok = mm(PS[nt][:, :], slot[:, 128 * cc:128 * cc + 128],
                             chsl(XBF, cc, nt), start=(cc == 0),
                             stop=(cc == NCH - 1),
                             waits=w, tag=f"{which}p{oc}_{cc}_{nt}")
                    last[nt] = tok
                    reads.append(tok)
            slot_readers[par] = reads
            qp = (QPRE, QPRE2)[par]
            for nt in range(NNT):
                cp_ev = em("s", lambda s, nt=nt, qp=qp: s.copy(
                    out=qp[:, 512 * nt:512 * nt + 512], in_=PS[nt][:, :]),
                    waits=[last[nt]] + state["qpre_readers"][par][nt],
                    tag=f"{which}ev{oc}_{nt}")
                state["qpre_readers"][par][nt] = []
                state["bank_free"][nt] = cp_ev
                rbk = 6 + nt % 2
                rot = mm(PS[rbk][0:128, :], R2TB[:],
                         qp[:, 512 * nt:512 * nt + 512], start=True, stop=True,
                         waits=[cp_ev, cp_r2tb, state["rot_free"][rbk]],
                         tag=f"{which}rot{oc}_{nt}")
                tt1 = em("v", lambda v, nt=nt, qp=qp: v.tensor_tensor(
                    out=T0A[:], in0=qp[:, 512 * nt:512 * nt + 512],
                    in1=COS[:, 512 * nt:512 * nt + 512], op=ALU.mult),
                    waits=[cp_ev, state["t0_reader"]] +
                          (cp_fold if oc == 0 else []),
                    tag=f"{which}t1_{oc}_{nt}")
                state["qpre_readers"][par][nt].append(tt1)
                tt2 = em("v", lambda v, rbk=rbk, nt=nt: v.tensor_tensor(
                    out=T0B[:], in0=PS[rbk][0:128, :],
                    in1=SIN[:, 512 * nt:512 * nt + 512], op=ALU.mult),
                    waits=[rot], tag=f"{which}t2_{oc}_{nt}")
                state["rot_free"][rbk] = tt2
                if which == "q":
                    tokq = em("v", lambda v, oc=oc, nt=nt: v.tensor_tensor(
                        out=chsl(QT, oc, nt), in0=T0A[:], in1=T0B[:], op=ALU.add),
                        waits=[tt1, tt2, cp_spill[oc // 2]],
                        tag=f"qsum{oc}_{nt}")
                    state["t0_reader"] = tokq
                    out_tokens.append(tokq)
                else:
                    kslot = nt % 2
                    kr = em("v", lambda v, kslot=kslot: v.tensor_tensor(
                        out=PBUF[kslot][:], in0=T0A[:], in1=T0B[:], op=ALU.add),
                        waits=[tt1, tt2, state["bil_free"][kslot]],
                        tag=f"krop{oc}_{nt}")
                    state["t0_reader"] = kr
                    b0 = mm(PS[nt][0:64, :], WBB[0:64, 64 * oc:64 * oc + 64],
                            PBUF[kslot][0:64, :], start=True, stop=True,
                            waits=[kr, cp_ev] + (cp_wbs if oc == 0 else []),
                            tp=(0, 0), tag=f"bil0_{oc}_{nt}")
                    b1 = mm(PS[nt][64:128, :], WBB[64:128, 64 * oc:64 * oc + 64],
                            PBUF[kslot][64:128, :], start=True, stop=True,
                            waits=[kr], tp=(64, 64), tag=f"bil1_{oc}_{nt}")
                    state["bil_free"][kslot] = b1
                    ktev = em("s", lambda s, oc=oc, nt=nt: s.copy(
                        out=chsl(KTT, oc, nt), in_=PS[nt][0:128, :]),
                        waits=[b0, b1, cp_spill[4 + oc // 2]],
                        tag=f"ktev{oc}_{nt}")
                    state["bank_free"][nt] = ktev
                    out_tokens.append(ktev)
            if which == "q" and oc == 0 and "QPRE6" in debug:
                em("g", lambda g: g.dma_start(out=dbg_h["QPRE6"][:], in_=QPRE[:]),
                   waits=[state["bank_free"][n] for n in range(NNT)])
            if which == "q" and oc == 0 and "WQS6" in debug:
                em("g", lambda g, slot=slot: g.dma_start(
                    out=dbg_h["WQS6"][:], in_=slot[:]), waits=slot_readers[par])
            if which == "q" and oc == 0 and "WST6" in debug:
                em("g", lambda g: g.dma_start(
                    out=dbg_h["WST6"][:], in_=WST[:]), waits=[casts[-1]])
        return out_tokens

    cp_q = proj_qk("q", [])
    cp_k = proj_qk("k", [])

    if "QT" in debug:
        em("g", lambda g: g.dma_start(out=dbg_h["QT"][:], in_=QT[:]), waits=cp_q)
    if "KTT" in debug:
        em("g", lambda g: g.dma_start(out=dbg_h["KTT"][:], in_=KTT[:]), waits=cp_k)
    if "WBB" in debug:
        em("g", lambda g: g.dma_start(out=dbg_h["WBB"][:], in_=WBB[:]),
           waits=cp_k)
    if upto == "C":
        return nc, p, dbg_h

    cp_wv_cast = []
    stage_readers = {0: list(stage_state[0]), 1: list(stage_state[1])}
    for cc in range(NCH):
        par = cc % 2
        half = par * 1024
        cp_d = em("y", lambda y, cc=cc, half=half: y.dma_start(
            out=WST[:, half:half + 1024], in_=wv_h[cc]),
            waits=stage_readers[par], tag=f"wvd{cc}", dgroup=f"wv{cc // 2}")
        cst = em("s", lambda s, cc=cc, half=half: s.mul(
            out=WVB[:, 1024 * cc:1024 * cc + 1024],
            in_=WST[:, half:half + 1024],
            mul=GAMC[:, cc:cc + 1]), waits=[cp_d], tag=f"wvc{cc}")
        stage_readers[par] = [cst]
        cp_wv_cast.append(cst)
    ones_ap = VT[:, 0:16640].rearrange(
        "p (tc h e) -> p tc h e", h=16, e=65)[:, :, :, 64:65]
    cp_vones = em("v", lambda v: v.memset(ones_ap, 1.0), waits=[cp_q[-1]],
                  tag="vones")
    cp_vt = [cp_vones]
    vev_free = {i: cp_k[-1] for i in range(4)}
    for tc in range(NTC):
        b = (tc % 2) * 2
        last = [None, None]
        for cc in range(NCH):
            for h2 in range(2):
                w = [cp_wv_cast[cc]]
                if cc == 0:
                    w.append(vev_free[b + h2])
                last[h2] = mm(PS[b + h2][:, :],
                              XBF[:, 2048 * cc + 128 * tc:2048 * cc + 128 * tc + 128],
                              WVB[:, 1024 * cc + 512 * h2:1024 * cc + 512 * h2 + 512],
                              start=(cc == 0), stop=(cc == NCH - 1), waits=w,
                              tag=f"vp{tc}_{cc}_{h2}")
        for h2 in range(2):
            dst = VT[:, 1040 * tc: 1040 * tc + 1040].rearrange(
                "p (h e) -> p h e", e=65)[:, 8 * h2:8 * h2 + 8, 0:64]
            src = PS[b + h2][:, :].rearrange("p (h e) -> p h e", e=64)
            cp_ev = em("v", lambda v, dst=dst, src=src, tc=tc, h2=h2:
                       v.tensor_scalar_mul(
                out=dst, in0=src, scalar1=RSTDC[:, tc:tc + 1]),
                waits=[last[h2], cp_rstdc, cp_vones], tag=f"vev{tc}_{h2}")
            vev_free[b + h2] = cp_ev
            cp_vt.append(cp_ev)

    if "VT" in debug:
        em("g", lambda g: g.dma_start(out=dbg_h["VT"][:], in_=VT[:]), waits=cp_vt)
    if upto == "D":
        return nc, p, dbg_h

    red_q = [None] * NNT
    red_z = [None] * NNT
    npat_free = {0: None, 1: None, 2: None, 3: None}
    cp_q2 = [None] * NNT
    cp_z2 = [None] * NNT
    for src, base_ps, cps, idx in ((QT, 0, cp_q2, 0), (KTT, 4, cp_z2, 1)):
        for pr in range(NPR):
            for nt in range(NNT):
                slot = 2 * (nt % 2) + idx
                w = [cp_q[-1], cp_k[-1], cp_eptb, npat_free[slot]]
                if pr == 0 and nt == 0 and idx == 0:
                    w += cp_vt
                sq = em("v", lambda v, src=src, pr=pr, nt=nt, slot=slot:
                        v.tensor_tensor(out=PBUF[slot][:], in0=chsl(src, pr, nt),
                                        in1=chsl(src, pr, nt), op=ALU.mult),
                        waits=w, tag=f"esq{idx}_{pr}_{nt}")
                tok = mm(PS[base_ps + nt][0:8, :], EPATB[:, 8 * pr:8 * pr + 8],
                         PBUF[slot][:], start=(pr == 0), stop=(pr == NPR - 1),
                         waits=[sq] + ([cp_vt[-1]] if pr == 0 and idx == 1 else []),
                         tag=f"ept{idx}_{pr}_{nt}")
                npat_free[slot] = tok
                cps[nt] = tok
    for nt in range(NNT):
        red_q[nt] = em("v", lambda v, nt=nt: v.reduce_max(
            out=Q2M[0:8, nt:nt + 1], in_=PS[nt][0:8, :], axis=AX),
            waits=[cp_q2[nt]], tag=f"redq{nt}")
        red_z[nt] = em("v", lambda v, nt=nt: v.reduce_max(
            out=Z2M[0:8, nt:nt + 1], in_=PS[4 + nt][0:8, :], axis=AX),
            waits=[cp_z2[nt]], tag=f"redz{nt}")
    q1 = em("v", lambda v: v.tensor_tensor(out=Q2M[0:8, 0:1], in0=Q2M[0:8, 0:1],
                                           in1=Q2M[0:8, 1:2], op=ALU.max),
            waits=[red_q[3]])
    q2 = em("v", lambda v: v.tensor_tensor(out=Q2M[0:8, 2:3], in0=Q2M[0:8, 2:3],
                                           in1=Q2M[0:8, 3:4], op=ALU.max),
            waits=[red_q[2], red_q[3]])
    q3 = em("v", lambda v: v.tensor_tensor(out=Q2M[0:8, 0:1], in0=Q2M[0:8, 0:1],
                                           in1=Q2M[0:8, 2:3], op=ALU.max),
            waits=[q1, q2])
    z1 = em("v", lambda v: v.tensor_tensor(out=Z2M[0:8, 0:1], in0=Z2M[0:8, 0:1],
                                           in1=Z2M[0:8, 1:2], op=ALU.max),
            waits=[red_z[3]])
    z2 = em("v", lambda v: v.tensor_tensor(out=Z2M[0:8, 2:3], in0=Z2M[0:8, 2:3],
                                           in1=Z2M[0:8, 3:4], op=ALU.max),
            waits=[red_z[2], red_z[3]])
    z3 = em("v", lambda v: v.tensor_tensor(out=Z2M[0:8, 0:1], in0=Z2M[0:8, 0:1],
                                           in1=Z2M[0:8, 2:3], op=ALU.max),
            waits=[z1, z2])
    e1 = em("v", lambda v: v.tensor_tensor(out=CP8[0:8, 0:1], in0=Q2M[0:8, 0:1],
                                           in1=Z2M[0:8, 0:1], op=ALU.mult),
            waits=[q3, z3], tag="cpmul")
    e2 = em("s", lambda s: s.activation(out=CP8[0:8, 0:1], in_=CP8[0:8, 0:1],
                                        func=AF.Sqrt, scale=f32(1.0 / 64.0)),
            waits=[e1], tag="cpsqrt")
    tr = mm(PS[0][0:1, 0:8], CP8[0:8, 0:1], ID8[:], start=True, stop=True,
            waits=[e2, cp_id8], tag="cptr")
    e3 = em("s", lambda s: s.copy(out=CPROW[0:1, 0:8], in_=PS[0][0:1, 0:8]),
            waits=[tr], tag="cprow")
    br = mm(PS[1][:, 0:8], ONESR[0:1, 0:128], CPROW[0:1, 0:8], start=True,
            stop=True, waits=[e3], tag="cpbr")
    cp_cvec = em("s", lambda s: s.mul(out=CVEC[:], in_=PS[1][:, 0:8],
                                      mul=f32(-1.0)), waits=[br], tag="cvec")
    if "CVEC" in debug:
        em("g", lambda g: g.dma_start(out=dbg_h["CVEC"][:], in_=CVEC[:]),
           waits=[cp_cvec])

    PSAB3 = [PSA, PSB, PSD]
    expdone = {0: cp_cvec, 1: red_q[3], 2: red_z[3]}
    pvprev = {0: None, 1: None, 2: None, 3: None}
    o_free = {0: red_z[0], 1: red_z[1]}
    bounce_free = red_z[2]
    cp_aot = []
    cp_qk_last = None
    for pr in range(NPR):
        for qt in range(NNT):
            exptok = {}
            pv_last = [None, None]

            def emit_pv(kc, hh, pr=pr, qt=qt, pv_last=pv_last, exptok=exptok):
                wpv = [exptok[kc]]
                if kc == 0:
                    wpv.append(o_free[hh])
                pv = mm(PSC[0:65, 512 * hh:512 * hh + 512],
                        VT[:, 1040 * kc + 65 * (2 * pr + hh):
                           1040 * kc + 65 * (2 * pr + hh) + 65],
                        PBUF[2 * (kc % 4) + hh][:], start=(kc == 0),
                        stop=(kc == NTC - 1),
                        waits=wpv, tag=f"pv{pr}_{qt}_{kc}_{hh}")
                pv_last[hh] = pv
                return pv

            for kc in range(NTC):
                par = kc % 3
                slot4 = kc % 4
                qks = []
                for hh in range(2):
                    base = 64 * hh
                    qk = mm(PSAB3[par][:, 512 * hh:512 * hh + 512],
                            KTT[base:base + 64,
                                2048 * pr + 128 * kc:2048 * pr + 128 * kc + 128],
                            QT[base:base + 64,
                               2048 * pr + 512 * qt:2048 * pr + 512 * qt + 512],
                            start=True, stop=True,
                            waits=[expdone[par]],
                            tag=f"qk{pr}_{qt}_{kc}_{hh}")
                    qks.append(qk)
                    cp_qk_last = qk
                pvw = [pvprev[slot4]]
                if kc >= 4:
                    pvw = [emit_pv(kc - 4, 0), emit_pv(kc - 4, 1)]
                ex = em("s", lambda s, slot4=slot4, par=par, pr=pr: s.activation(
                    out=PBIG[slot4][:], in_=PSAB3[par][:, :], func=AF.Exp,
                    scale=f32(0.125), bias=CVEC[:, pr:pr + 1]),
                    waits=qks + [cp_cvec] + pvw,
                    tag=f"ex{pr}_{qt}_{kc}")
                exptok[kc] = ex
                expdone[par] = ex
            for kc in (NTC - 4, NTC - 3, NTC - 2, NTC - 1):
                emit_pv(kc, 0)
                pvprev[kc % 4] = emit_pv(kc, 1)
            rl = em("s", lambda s: s.activation(
                out=BOUNCE[0:1, :], in_=PSC[64:65, 0:1024], func=AF.Ln),
                waits=[pv_last[0], pv_last[1], bounce_free],
                tag=f"rl{pr}_{qt}")
            rc = em("s", lambda s: s.activation(
                out=BOUNCE[0:1, :], in_=BOUNCE[0:1, :], func=AF.Exp,
                scale=f32(-1.0)), waits=[rl], tag=f"rc{pr}_{qt}")
            nbr0 = mm(PSD[0:64, 0:512], ONESR[0:1, 0:64], BOUNCE[0:1, 0:512],
                      start=True, stop=True, waits=[rc, expdone[2]],
                      tag=f"nbr0{pr}_{qt}")
            nbr1 = mm(PSD[0:64, 512:1024], ONESR[0:1, 0:64],
                      BOUNCE[0:1, 512:1024], start=True, stop=True,
                      waits=[rc], tag=f"nbr1{pr}_{qt}")
            cpc = em("v", lambda v: v.tensor_copy(
                out=T0W[0:64, :], in_=PSD[0:64, 0:1024]),
                waits=[nbr0, nbr1, o_free[0]], tag=f"ncp{pr}_{qt}")
            bounce_free = cpc
            expdone[2] = cpc
            for hh in range(2):
                tt = em("v", lambda v, hh=hh, pr=pr, qt=qt: v.tensor_tensor(
                    out=AOT[64 * hh:64 * hh + 64,
                            2048 * pr + 512 * qt:2048 * pr + 512 * qt + 512],
                    in0=PSC[0:64, 512 * hh:512 * hh + 512],
                    in1=T0W[0:64, 512 * hh:512 * hh + 512], op=ALU.mult),
                    waits=[cpc], tag=f"ntt{pr}_{qt}_{hh}")
                o_free[hh] = tt
                cp_aot.append(tt)

    if "AOT" in debug:
        em("g", lambda g: g.dma_start(out=dbg_h["AOT"][:], in_=AOT[:]),
           waits=cp_aot)
    if upto == "E":
        return nc, p, dbg_h

    cp_rel = [em("g", lambda g, ch=ch: g.dma_start(
        out=chsl(XT2, ch), in_=spill_h[128 * ch:128 * ch + 128, :]),
        waits=[cp_qk_last], tag=f"rel{ch}", dgroup=f"rl{ch // 2}")
        for ch in range(NCH)]
    cp_r1 = []
    stageF = {0: [cp_wv_cast[-1]], 1: [cp_wv_cast[-1]]}
    slotF = {0: [], 1: []}
    fev_free = [None] * NNT
    for dm in range(NCH):
        par = dm % 2
        slot = (WQS0, WQS1)[par]
        cp_d = em("y", lambda y, dm=dm, par=par: y.dma_start(
            out=WST[:, par * 1024:par * 1024 + 1024].rearrange(
                "p (c j) -> p c j", j=128),
            in_=wo_h[dm].rearrange("c p j -> p c j")),
            waits=stageF[par] + slotF[par] + (cp_q + cp_k if dm < 2 else []),
            tag=f"wod{dm}", dgroup=f"wo{dm // 2}")
        cst = em("v", lambda v, slot=slot, par=par: v.tensor_copy(
            out=slot[:], in_=WST[:, par * 1024:par * 1024 + 1024]),
            waits=[cp_d], tag=f"woc{dm}")
        stageF[par] = [cst]
        reads = []
        for nt in range(NNT):
            last = None
            for ic in range(NCH):
                w = [cst]
                if ic == 0:
                    w.append(fev_free[nt])
                    if dm == 0:
                        w.append(cp_aot[-1])
                last = mm(PS[nt][:, :], slot[:, 128 * ic:128 * ic + 128],
                          chsl(AOT, ic, nt), start=(ic == 0),
                          stop=(ic == NCH - 1), waits=w, tag=f"wo{dm}_{ic}_{nt}")
                reads.append(last)
            ev = em("v", lambda v, dm=dm, nt=nt: v.tensor_tensor(
                out=chsl(XT2, dm, nt), in0=PS[nt][:, :], in1=chsl(XT2, dm, nt),
                op=ALU.add), waits=[last, cp_rel[dm]], tag=f"fev{dm}_{nt}")
            fev_free[nt] = ev
            cp_r1.append(ev)
        slotF[par] = reads

    if "R1" in debug:
        em("g", lambda g: g.dma_start(out=dbg_h["R1"][:], in_=XT2[:]),
           waits=cp_r1)
    if upto == "F":
        return nc, p, dbg_h

    def layernorm_inplace(prev_last, murow, rstdrow, tag):
        sq2 = [[None] * NNT for _ in range(NCH)]
        for ch in range(NCH):
            for nt in range(NNT):
                w = [prev_last] if ch == 0 else [sq2[ch - 1][nt]]
                sq = em("s", lambda s, ch=ch, nt=nt: s.square(
                    out=PBUF[nt][:], in_=chsl(XT2, ch, nt)), waits=w,
                    tag=f"{tag}cb{ch}{nt}")
                sx = mm(PS[nt][0:1, :], ONESCF[:, 0:1], chsl(XT2, ch, nt),
                        start=(ch == 0), stop=(ch == NCH - 1),
                        waits=[None if ch else prev_last, cp_onescf],
                        tag=f"{tag}sx{ch}{nt}")
                sq2[ch][nt] = mm(PS[4 + nt][0:1, :], ONESCB[:, 0:1], PBUF[nt][:],
                                 start=(ch == 0), stop=(ch == NCH - 1),
                                 waits=[sq], tag=f"{tag}sq{ch}{nt}")
        cp_mu = [None] * NNT
        cp_rs = [None] * NNT
        c3prev = None
        for nt in range(NNT):
            wstat = [sq2[NCH - 1][nt]]
            cp_mu[nt] = em("s", lambda s, nt=nt, murow=murow: s.mul(
                out=murow[0:1, 512 * nt:512 * nt + 512], in_=PS[nt][0:1, :],
                mul=f32(-1.0 / C)), waits=wstat, tag=f"{tag}mu{nt}")
            a1 = em("v", lambda v, nt=nt, murow=murow: v.tensor_tensor(
                out=T0A[0:1, :], in0=murow[0:1, 512 * nt:512 * nt + 512],
                in1=murow[0:1, 512 * nt:512 * nt + 512], op=ALU.mult),
                waits=[cp_mu[nt]], tag=f"{tag}a1{nt}")
            a2 = em("s", lambda s, nt=nt: s.mul(
                out=T0B[0:1, :], in_=PS[4 + nt][0:1, :], mul=f32(1.0 / C)),
                waits=wstat + [c3prev], tag=f"{tag}a2{nt}")
            a3 = em("v", lambda v: v.tensor_tensor(
                out=T0B[0:1, :], in0=T0B[0:1, :], in1=T0A[0:1, :],
                op=ALU.subtract), waits=[a1, a2], tag=f"{tag}a3{nt}")
            a4 = em("s", lambda s: s.activation(
                out=T0B[0:1, :], in_=T0B[0:1, :], func=AF.Sqrt,
                bias=f32(EPS)), waits=[a3], tag=f"{tag}a4{nt}")
            cp_rs[nt] = em("v", lambda v, nt=nt, rstdrow=rstdrow: v.reciprocal(
                out=rstdrow[0:1, 512 * nt:512 * nt + 512], in_=T0B[0:1, :]),
                waits=[a4], tag=f"{tag}rs{nt}")
            c3prev = cp_rs[nt]
        ap_last = []
        ap_nt = [None] * NNT
        ap_ch = [None] * NCH
        bank_free = {0: cp_mu[0], 1: cp_mu[1], 2: cp_mu[2], 3: cp_mu[3]}
        for nt in range(NNT):
            b = 2 * (nt % 2)
            mub = mm(PS[b][:, :], ONESR[0:1, 0:128],
                     murow[0:1, 512 * nt:512 * nt + 512], start=True, stop=True,
                     waits=[cp_mu[nt], bank_free[b]], tag=f"{tag}mub{nt}")
            rsb = mm(PS[b + 1][:, :], ONESR[0:1, 0:128],
                     rstdrow[0:1, 512 * nt:512 * nt + 512], start=True,
                     stop=True, waits=[cp_rs[nt], bank_free[b + 1]],
                     tag=f"{tag}rsb{nt}")
            s1 = s2 = None
            for ch in range(NCH):
                s1 = em("v", lambda v, ch=ch, nt=nt, b=b: v.tensor_tensor(
                    out=chsl(XT2, ch, nt), in0=chsl(XT2, ch, nt),
                    in1=PS[b][:, :], op=ALU.add), waits=[mub],
                    tag=f"{tag}p1_{ch}_{nt}")
                s2 = em("v", lambda v, ch=ch, nt=nt, b=b: v.tensor_tensor(
                    out=chsl(XT2, ch, nt), in0=chsl(XT2, ch, nt),
                    in1=PS[b + 1][:, :], op=ALU.mult), waits=[rsb, s1],
                    tag=f"{tag}p2_{ch}_{nt}")
                ap_last.append(s2)
                ap_ch[ch] = s2
            ap_nt[nt] = s2
            bank_free[b] = s1
            bank_free[b + 1] = s2
        return ap_last, ap_nt, ap_ch

    cp_y, cp_y_nt, _ = layernorm_inplace(cp_r1[-1], MU2, RSTD2, "ga")
    if "Y" in debug:
        em("g", lambda g: g.dma_start(out=dbg_h["Y"][:], in_=XT2[:]), waits=cp_y)
    if upto == "G":
        return nc, p, dbg_h

    def mix(dst, mucol, mu1col, cp_mcol, cp_mcol2, prev_nt, tag):
        cps = []
        cps_nt = [None] * NNT
        for nt in range(NNT):
            for ch in range(NCH):
                base = 2048 * ch + 512 * nt
                lo = 1 if nt == 0 else 0
                t1 = em("v", lambda v, base=base, ch=ch, dst=dst,
                        mucol=mucol: v.tensor_scalar_mul(
                    out=dst[:, base:base + 512], in0=XT2[:, base:base + 512],
                    scalar1=mucol[:, ch:ch + 1]),
                    waits=[prev_nt[nt], cp_mcol], tag=f"{tag}a{ch}_{nt}")
                t2 = em("v", lambda v, base=base, lo=lo, ch=ch, dst=dst,
                        mu1col=mu1col: v.scalar_tensor_tensor(
                    out=dst[:, base + lo:base + 512],
                    in0=XT2[:, base + lo - 1:base + 511],
                    scalar=mu1col[:, ch:ch + 1],
                    in1=dst[:, base + lo:base + 512],
                    op0=ALU.mult, op1=ALU.add), waits=[t1, cp_mcol2],
                    tag=f"{tag}m{ch}_{nt}")
                cps.append(t2)
            cps_nt[nt] = cps[-1]
        return cps, cps_nt

    cp_xk, cp_xk_nt = mix(XK, MUKC, MUKC2, cp_muk, cp_muk2, cp_y_nt, "mk")
    if "XK" in debug:
        em("g", lambda g: g.dma_start(out=dbg_h["XK"][:], in_=XK[:]),
           waits=cp_xk)

    cp_kact = []
    cp_kact_ntd = [None] * NNT
    stageH = {0: [stageF[0][-1]], 1: [stageF[1][-1]]}
    slotH = {0: list(slotF[0]), 1: list(slotF[1])}
    t0_free = {0: None, 1: None}
    bank_freeH = [None] * NNT
    for dm in range(NCH):
        par = dm % 2
        slot = (KWB0, KWB1)[par]
        cp_d = em("y", lambda y, dm=dm, par=par: y.dma_start(
            out=WST[:, par * 1024:par * 1024 + 1024].rearrange(
                "p (c j) -> p c j", j=128),
            in_=wkf_h[dm].rearrange("c p j -> p c j")),
            waits=stageH[par] + slotH[par],
            tag=f"kwd{dm}", dgroup=f"kw{dm // 2}")
        cst = em("v", lambda v, slot=slot, par=par: v.tensor_copy(
            out=slot[:], in_=WST[:, par * 1024:par * 1024 + 1024]),
            waits=[cp_d], tag=f"kwc{dm}")
        stageH[par] = [cst]
        reads = []
        for nt in range(NNT):
            last = None
            for cc in range(NCH):
                w = [cst]
                if cc == 0:
                    w.append(bank_freeH[nt])
                    w.append(cp_xk_nt[nt])
                last = mm(PS[nt][:, :], slot[:, 128 * cc:128 * cc + 128],
                          chsl(XK, cc, nt), start=(cc == 0),
                          stop=(cc == NCH - 1), waits=w, tag=f"kp{dm}_{cc}_{nt}")
                reads.append(last)
            tp = nt % 2
            t0 = (T0A, T0B)[tp]
            r1 = em("s", lambda s, t0=t0, nt=nt: s.activation(
                out=t0[:], in_=PS[nt][:, :], func=AF.Relu),
                waits=[last, t0_free[tp]], tag=f"kr{dm}_{nt}")
            bank_freeH[nt] = r1
            ks = em("p", lambda g, dm=dm, nt=nt, t0=t0: g.tensor_tensor(
                out=chsl(KACT, dm, nt), in0=t0[:], in1=t0[:], op=ALU.mult),
                waits=[r1], tag=f"ks{dm}_{nt}")
            t0_free[tp] = ks
            cp_kact.append(ks)
            cp_kact_ntd[nt] = ks
        slotH[par] = reads
    if "KACT" in debug:
        em("g", lambda g: g.dma_start(out=dbg_h["KACT"][:], in_=KACT[:]),
           waits=cp_kact)

    cp_xr, cp_xr_nt = mix(XR, MURC, MURC2, cp_mur, cp_mur2, cp_kact_ntd, "mr")
    if "XR" in debug:
        em("g", lambda g: g.dma_start(out=dbg_h["XR"][:], in_=XR[:]),
           waits=cp_xr)

    cp_r2 = []
    wst_r = [stageH[0][-1], stageH[1][-1]]
    wst_v = [stageH[0][-1], stageH[1][-1]]
    rwb_readers = []
    vwb_readers = []
    t0a_free = None
    t0b_free = None
    bank4 = {0: None, 1: None, 2: None, 3: None}
    for dm in range(NCH):
        cp_dr = em("y", lambda y, dm=dm: y.dma_start(
            out=WST[:, 0:1024].rearrange("p (c j) -> p c j", j=128),
            in_=wr_h[dm].rearrange("c p j -> p c j")), waits=wst_r + rwb_readers, tag=f"rwd{dm}", dgroup=f"h4w{dm}")
        cr = em("s", lambda s: s.copy(out=RWB[:], in_=WST[:, 0:1024]),
                waits=[cp_dr], tag=f"rwc{dm}")
        wst_r = [cr]
        cp_dv = em("y", lambda y, dm=dm: y.dma_start(
            out=WST[:, 1024:2048].rearrange("p (c j) -> p c j", j=128),
            in_=wvf_h[dm].rearrange("c p j -> p c j")), waits=wst_v + vwb_readers, tag=f"vwd{dm}", dgroup=f"h4w{dm}")
        cv = em("v", lambda v: v.tensor_copy(out=VWB[:], in_=WST[:, 1024:2048]),
                waits=[cp_dv], tag=f"vwc{dm}")
        wst_v = [cv]
        rwb_readers = []
        vwb_readers = []
        for nt in range(NNT):
            b = nt % 2
            lastr = lastv = None
            for cc in range(NCH):
                w1 = [cr]
                w2 = [cv]
                if cc == 0:
                    w1.append(bank4[b])
                    w2.append(bank4[2 + b])
                    w1.append(cp_xr_nt[nt])
                lastr = mm(PS[b][:, :], RWB[:, 128 * cc:128 * cc + 128],
                           chsl(XR, cc, nt), start=(cc == 0),
                           stop=(cc == NCH - 1), waits=w1,
                           tag=f"rp{dm}_{cc}_{nt}")
                lastv = mm(PS[2 + b][:, :], VWB[:, 128 * cc:128 * cc + 128],
                           chsl(KACT, cc, nt), start=(cc == 0),
                           stop=(cc == NCH - 1), waits=w2,
                           tag=f"v2{dm}_{cc}_{nt}")
                rwb_readers.append(lastr)
                vwb_readers.append(lastv)
            sg = em("s", lambda s, b=b: s.activation(
                out=T0A[:], in_=PS[b][:, :], func=AF.Sigmoid),
                waits=[lastr, t0a_free], tag=f"sg{dm}_{nt}")
            bank4[b] = sg
            g1 = em("v", lambda v, b=b: v.tensor_tensor(
                out=T0B[:], in0=PS[2 + b][:, :], in1=T0A[:], op=ALU.mult),
                waits=[sg, lastv, t0b_free], tag=f"g1{dm}_{nt}")
            t0a_free = g1
            bank4[2 + b] = g1
            rr = em("v", lambda v, dm=dm, nt=nt: v.tensor_tensor(
                out=chsl(XT2, dm, nt), in0=T0B[:], in1=chsl(XT2, dm, nt),
                op=ALU.add), waits=[g1], tag=f"r2_{dm}_{nt}")
            t0b_free = rr
            cp_r2.append(rr)
    if upto == "H":
        pass

    cp_out, _, cp_out_ch = layernorm_inplace(cp_r2[-1], MU2, RSTD2, "gb")
    for ch in range(NCH):
        em("g", lambda g, ch=ch: g.dma_start(
            out=out_h[128 * ch:128 * ch + 128, :], in_=chsl(XT2, ch)),
            waits=[cp_out_ch[ch]], tag=f"odma{ch}")
    return nc, p, dbg_h


def finalize(nc, p):
    from contextlib import ExitStack
    refs = set()
    for e in p.ops:
        for fn, ws, key, c, tag, dg in p.ops[e]:
            for we, wv in ws:
                refs.add((we, wv))
    keymap = {}
    misc_total = 0
    for q in ("y", "g"):
        open_group = {}
        for op in p.ops[q]:
            if len(op) < 6:
                continue
            key, dg = op[2], op[5]
            if key in ("p",):
                continue
            if (key, 16) not in refs:
                keymap[key] = "dmisc"
                misc_total += 16
                op[3] = misc_total
            elif dg is not None and dg in open_group:
                lead = open_group.pop(dg)
                keymap[key] = lead
                op[3] = 32
            else:
                keymap[key] = key
                if dg is not None:
                    open_group[dg] = key
    eng_refs = {k: set() for k in ("t", "v", "s", "p")}
    for we, wv in refs:
        if we in eng_refs:
            eng_refs[we].add(wv)
    remap = {k: {v: i + 1 for i, v in enumerate(sorted(eng_refs[k]))}
             for k in eng_refs}
    dma_keys = sorted({keymap[op[2]] for q in ("y", "g") for op in p.ops[q]
                       if op[2] != "p"})
    finals = {}
    for q in ("y", "g"):
        for op in p.ops[q]:
            if op[2] == "p":
                continue
            k = keymap[op[2]]
            finals[k] = max(finals.get(k, 0), op[3])
    with ExitStack() as st:
        sems = {k: st.enter_context(nc.semaphore(f"sem_{k}"))
                for k in ["t", "v", "s", "p"] + dma_keys}
        block = st.enter_context(nc.Block())

        def rwait(eng, hw, we, wv):
            k = keymap.get(we, we)
            if we in remap:
                rv = remap[we][wv]
            else:
                rv = finals[k]
            if hw.get(k, 0) < rv:
                eng.wait_ge(sems[k], rv)
                hw[k] = rv

        def make_runner(e):
            ops = p.ops[e]

            def run(eng):
                hw = {}
                for fn, ws, key, c, tag, dg in ops:
                    for we, wv in ws:
                        rwait(eng, hw, we, wv)
                    ins = fn(eng)
                    if key in ("t", "v", "s", "p"):
                        if c in remap[key] and ins is not None:
                            ins.then_inc(sems[key], 1)
                    else:
                        ins.then_inc(sems[keymap[key]], 16)
                if e == "g":
                    for k, v in sorted(finals.items()):
                        if hw.get(k, 0) < v:
                            eng.wait_ge(sems[k], v)
            return run

        block.tensor(make_runner("t"))
        block.vector(make_runner("v"))
        block.scalar(make_runner("s"))
        block.sync(make_runner("y"))
        block.gpsimd(make_runner("g"))
    return nc


def host_inputs(full):
    x = np.ascontiguousarray(full["L0_attn_x"], dtype=np.float32)
    Wq = np.ascontiguousarray(full["L0_attn_Wq"], dtype=np.float32)
    Wkv = np.ascontiguousarray(full["L0_attn_Wkv"], dtype=np.float32)
    Wb = np.ascontiguousarray(full["L0_attn_W_bilinear"], dtype=np.float32)
    Wo = np.ascontiguousarray(full["L0_attn_Wo"], dtype=np.float32)
    Wr = np.ascontiguousarray(full["L0_ffn_W_r"], dtype=np.float32)
    Wkf = np.ascontiguousarray(full["L0_ffn_W_k"], dtype=np.float32)
    Wvf = np.ascontiguousarray(full["L0_ffn_W_v"], dtype=np.float32)
    gam = np.ascontiguousarray(full["L0_attn_gamma"], dtype=np.float32)
    mur = np.ascontiguousarray(full["L0_ffn_mu_r"], dtype=np.float32)
    muk = np.ascontiguousarray(full["L0_ffn_mu_k"], dtype=np.float32)

    def dm_tiles(W):
        return np.ascontiguousarray(
            W.reshape(8, 128, 8, 128).transpose(2, 0, 1, 3))

    cos2, sin2 = _rope_tables()
    shared = {
        "wq": dm_tiles(Wq),
        "wk": dm_tiles(Wkv[:, 0:C]),
        "wv": np.ascontiguousarray(Wkv[:, C:2 * C].reshape(8, 128, 1024)),
        "wo": dm_tiles(Wo),
        "wb": np.ascontiguousarray(Wb.reshape(8, 128, 64)),
        "wr": dm_tiles(np.ascontiguousarray(Wr.T)),
        "wkf": dm_tiles(np.ascontiguousarray(Wkf.T)),
        "wvf": dm_tiles(np.ascontiguousarray(Wvf.T)),
        "gam": np.ascontiguousarray(gam.reshape(8, 128).T),
        "mur": np.ascontiguousarray(mur.reshape(8, 128).T),
        "muk": np.ascontiguousarray(muk.reshape(8, 128).T),
        "cos2": cos2,
        "sin2": sin2,
        "r2t": _perm_r2t(),
        "epat": _epat(),
        "id8": np.eye(8, dtype=np.float32),
        "onesr": np.ones((1, 128), dtype=np.float32),
    }
    in_maps = []
    for c in range(8):
        m = dict(shared)
        m["xt"] = np.ascontiguousarray(x[c].T)
        in_maps.append(m)
    return in_maps


_NC_CACHE = {}


def kernel(**inputs):
    if "main" not in _NC_CACHE:
        nc, p, dbg = build_nc()
        _NC_CACHE["main"] = finalize(nc, p)
    nc = _NC_CACHE["main"]
    in_maps = host_inputs(inputs)
    res = run_bass_kernel_spmd(nc, in_maps, core_ids=list(range(8)))
    outs = [np.ascontiguousarray(r["out"].T) for r in res.results]
    return np.stack(outs, axis=0)
